# version 01
# baseline (speedup 1.0000x reference)
import numpy as np
from contextlib import ExitStack
import concourse.bass as bass
import concourse.mybir as mybir
from concourse.bass_utils import run_bass_kernel_spmd

F32 = mybir.dt.float32
BF16 = mybir.dt.bfloat16
AF = mybir.ActivationFunctionType
ALU = mybir.AluOpType

NCORES = 8
B, T, D = 4, 4096, 2048
DC = D // 128
TOWN = 2048
LB = 160
XCOLS = 1 + LB + TOWN
PASS = 1024
NPASS = TOWN // PASS
EXT = LB + PASS + 1
EPS = 1e-5
DEBUG = False


class K:
    ENG = ("T", "V", "S", "G", "Y")

    def __init__(self, name):
        self.nc = bass.Bass("TRN2", target_bir_lowering=False)
        self.name = name
        self.ex = ExitStack()
        self.q = {e: [] for e in self.ENG}
        self.sems = {}
        self.counts = {}
        self.dram = {}
        self.outs = []
        self.sbuf_bytes = 0

    def sem(self, name):
        if name not in self.sems:
            self.sems[name] = self.ex.enter_context(self.nc.semaphore(name))
            self.counts[name] = 0
        return self.sems[name]

    def cnt(self, name):
        self.sem(name)
        return self.counts[name]

    def _bump(self, name, n):
        self.sem(name)
        self.counts[name] += n
        return self.counts[name]

    def din(self, name, shape, dtype=F32):
        t = self.nc.dram_tensor(name, list(shape), dtype, kind="ExternalInput").ap()
        self.dram[name] = t
        return t

    def dout(self, name, shape, dtype=F32):
        t = self.nc.dram_tensor(name, list(shape), dtype, kind="ExternalOutput").ap()
        self.dram[name] = t
        self.outs.append(name)
        return t

    def dtmp(self, name, shape, dtype=F32):
        t = self.nc.dram_tensor(name, list(shape), dtype, kind="Internal").ap()
        self.dram[name] = t
        return t

    def sb(self, name, p, cols, dtype=F32):
        self.sbuf_bytes += cols * mybir.dt.size(dtype)
        return self.ex.enter_context(self.nc.sbuf_tensor(name, [p, cols], dtype)).ap()

    def ps(self, name, p, cols, dtype=F32):
        return self.ex.enter_context(self.nc.psum_tensor(name, [p, cols], dtype)).ap()

    ESEM = {"V": "vm", "S": "sm", "G": "gm"}

    def _wrap(self, eng, waits, inc, dma, body):
        waits = list(waits or [])
        if not dma and eng in self.ESEM:
            en = self.ESEM[eng]
            if inc is None:
                inc = (en, 1)
            elif not isinstance(inc, tuple):
                inc = (inc, 1)
            assert inc[0] == en or inc[0] in ("tm",), f"unexpected inc {inc} on {eng}"
            prev = self.cnt(en)
            if prev > 0:
                waits.append((en, prev))
        waits = [(self.sem(n), v) for (n, v) in waits]
        if inc is not None:
            if isinstance(inc, tuple):
                iname, n = inc
            else:
                iname, n = inc, (16 if dma else 1)
            ih = self.sem(iname)
            self._bump(iname, n)
            incv = (ih, n)
        else:
            incv = None

        def f(e, waits=waits, incv=incv, body=body):
            for (s, v) in waits:
                e.wait_ge(s, v)
            i = body(e)
            if incv is not None:
                i.then_inc(incv[0], incv[1])
        self.q[eng].append(f)

    def raw(self, eng, body, waits=None, inc=None, dma=False):
        self._wrap(eng, waits, inc, dma, body)

    def mm(self, out, lhsT, rhs, start, stop, waits=None, inc=None, tile_position=None):
        self._wrap("T", waits, inc, False,
                   lambda e, o=out, l=lhsT, r=rhs, st=start, sp=stop, tp=tile_position:
                   e.matmul(o, l, r, start=st, stop=sp, tile_position=tp))

    def transp(self, out, in_, ident, waits=None, inc=None):
        self._wrap("T", waits, inc, False,
                   lambda e, o=out, i=in_, d=ident: e.transpose(o, i, d))

    def tt(self, eng, out, a, b, op, waits=None, inc=None):
        self._wrap(eng, waits, inc, False,
                   lambda e, o=out, x=a, y=b, p=op: e.tensor_tensor(o, x, y, p))

    def ts(self, eng, out, in_, s1, s2, op0, op1=ALU.bypass, waits=None, inc=None):
        self._wrap(eng, waits, inc, False,
                   lambda e, o=out, i=in_, a=s1, b=s2, p0=op0, p1=op1:
                   e.tensor_scalar(o, i, a, b, p0, p1))

    def stt(self, eng, out, in0, scalar, in1, op0, op1, waits=None, inc=None):
        self._wrap(eng, waits, inc, False,
                   lambda e, o=out, i=in0, s=scalar, j=in1, p0=op0, p1=op1:
                   e.scalar_tensor_tensor(o, i, s, j, p0, p1))

    def act(self, out, in_, func, bias=0.0, scale=1.0, waits=None, inc=None):
        self._wrap("S", waits, inc, False,
                   lambda e, o=out, i=in_, f=func, b=bias, s=scale:
                   e.activation(o, i, f, bias=b, scale=s))

    def copy(self, eng, out, in_, waits=None, inc=None):
        self._wrap(eng, waits, inc, False, lambda e, o=out, i=in_: e.tensor_copy(o, i))

    def scan(self, out, d0, d1, init, op0, op1, waits=None, inc=None):
        self._wrap("V", waits, inc, False,
                   lambda e, o=out, a=d0, b=d1, i=init, p0=op0, p1=op1:
                   e.tensor_tensor_scan(o, a, b, i, p0, p1))

    def red(self, eng, out, in_, op, waits=None, inc=None):
        self._wrap(eng, waits, inc, False,
                   lambda e, o=out, i=in_, p=op:
                   e.tensor_reduce(o, i, mybir.AxisListType.X, p))

    def recip(self, out, in_, waits=None, inc=None):
        self._wrap("V", waits, inc, False, lambda e, o=out, i=in_: e.reciprocal(o, i))

    def memset(self, eng, ap, val, waits=None, inc=None):
        self._wrap(eng, waits, inc, False, lambda e, a=ap, v=val: e.memset(a, v))

    NDMASEM = 4

    def dma(self, eng, out, in_, waits=None, inc=None, slow=False):
        tag = {"G": "gd", "Y": "yd", "S": "sd", "V": "vd", "T": "td"}[eng]
        if not hasattr(self, "_rr"):
            self._rr = {}
        i = self._rr.get(tag, 0)
        self._rr[tag] = (i + 1) % self.NDMASEM
        sn = f"{tag}{i}"
        w = list(waits or [])
        if self.cnt(sn) > 0:
            w.append((sn, self.cnt(sn)))
        self._wrap(eng, w, (sn, 16), True,
                   lambda e, o=out, i_=in_, s=slow:
                   e.dma_start(out=o, in_=i_, allow_slow_non_contiguous=s))
        return (sn, self.cnt(sn))

    def dma_tags(self, tag):
        return [f"{tag}{i}" for i in range(self.NDMASEM)]

    def fence_tag(self, eng, tag):
        for sn in self.dma_tags(tag):
            if sn in self.sems:
                self.wait(eng, sn, self.cnt(sn))

    def wait(self, eng, name, value):
        h = self.sem(name)
        self.q[eng].append(lambda e, s=h, v=value: e.wait_ge(s, v))

    def fence(self, eng, *semnames):
        for n in semnames:
            self.wait(eng, n, self.cnt(n))

    def settle(self, eng="V", sem="vm"):
        if not hasattr(self, "_settle_buf"):
            self._settle_buf = self.sb("_settle", 1, 8, F32)
        self.copy(eng, self._settle_buf[0:1, 0:4], self._settle_buf[0:1, 4:8], inc=(sem, 1))
        return self.cnt(sem)

    def hard(self):
        names = [n for n in self.sems if self.counts[n] > 0]
        for e in self.ENG:
            for n in names:
                self.wait(e, n, self.cnt(n))

    def build(self):
        nc = self.nc
        with self.ex:
            with nc.Block() as blk:
                @blk.tensor
                def _(e):
                    for fn in self.q["T"]:
                        fn(e)

                @blk.vector
                def _(e):
                    for fn in self.q["V"]:
                        fn(e)

                @blk.scalar
                def _(e):
                    for fn in self.q["S"]:
                        fn(e)

                @blk.gpsimd
                def _(e):
                    for fn in self.q["G"]:
                        fn(e)

                @blk.sync
                def _(e):
                    for fn in self.q["Y"]:
                        fn(e)
        return nc


def chunks(ap):
    return ap.rearrange("(c p) t -> c p t", p=128)


def cview(sbuf_ap, c, w):
    return sbuf_ap[:, c * w:(c + 1) * w]



GW = 256

class LNKit:
    def __init__(self, k):
        self.ones_bf = k.sb("ln_ones", 128, 1, BF16)
        self.onerow_f = k.sb("ln_onerow", 1, 128, F32)
        self.sqf = k.sb("ln_sqf", 128, DC * GW, BF16)
        self.m = k.sb("ln_m", 1, GW, F32)
        self.rstd = k.sb("ln_rstd", 1, GW, F32)
        self.nmr = k.sb("ln_nmr", 1, GW, F32)
        self.tmp = k.sb("ln_tmp", 1, GW, F32)
        self.tmp2 = k.sb("ln_tmp2", 1, GW, F32)
        self.rb = k.sb("ln_rb", 128, GW, F32)
        self.nb = k.sb("ln_nb", 128, GW, F32)
        self.lnt = k.sb("ln_t", 128, GW, F32)
        self.ps_s = k.ps("ln_ps_s", 1, 2 * GW, F32)
        self.ps_b = k.ps("ln_ps_b", 128, 1024, F32)
        ones_in = k.dram["c_ones"]
        t1 = k.dma("G", self.ones_bf[:, 0:1], ones_in.rearrange("(p o) -> p o", p=128, o=1))
        t2 = k.dma("G", self.onerow_f[0:1, :], ones_in.rearrange("(o p) -> o p", o=1, p=128))
        k.wait("T", *t1)
        k.wait("T", *t2)


def emit_ln(k, ln, xc, gw, out_c, gamma=None, beta=None):
    assert gw <= GW
    k.fence("S", "tm")
    k.fence("S", "vm")
    k.fence("T", "vm")
    for c in range(DC):
        k.act(cview(ln.sqf, c, GW)[:, 0:gw], xc(c), AF.Square,
              inc=("sm", 1) if c == DC - 1 else None)
    smv = k.cnt("sm")
    vnow = k.cnt("vm")
    for c in range(DC):
        k.mm(ln.ps_s[0:1, 0:gw], ln.ones_bf[:, 0:1], xc(c),
             start=(c == 0), stop=(c == DC - 1),
             waits=[("vm", vnow)] if c == 0 else None)
    for c in range(DC):
        k.mm(ln.ps_s[0:1, GW:GW + gw], ln.ones_bf[:, 0:1], cview(ln.sqf, c, GW)[:, 0:gw],
             start=(c == 0), stop=(c == DC - 1),
             waits=[("sm", smv)] if c == 0 else None,
             inc=("tm", 1) if c == DC - 1 else None)
    tmv = k.cnt("tm")
    m, rstd, nmr, tmp, tmp2 = (ln.m[0:1, :], ln.rstd[0:1, :], ln.nmr[0:1, :],
                               ln.tmp[0:1, :], ln.tmp2[0:1, :])
    k.ts("V", m[:, 0:gw], ln.ps_s[0:1, 0:gw], 1.0 / D, None, ALU.mult,
         waits=[("tm", tmv)])
    k.ts("V", tmp[:, 0:gw], ln.ps_s[0:1, GW:GW + gw], 1.0 / D, None, ALU.mult)
    k.tt("V", tmp2[:, 0:gw], m[:, 0:gw], m[:, 0:gw], ALU.mult)
    k.tt("V", tmp[:, 0:gw], tmp[:, 0:gw], tmp2[:, 0:gw], ALU.subtract)
    k.ts("V", tmp[:, 0:gw], tmp[:, 0:gw], float(EPS), None, ALU.add, inc=("vm", 1))
    k.act(tmp2[:, 0:gw], tmp[:, 0:gw], AF.Ln, waits=[("vm", k.cnt("vm"))])
    k.act(rstd[:, 0:gw], tmp2[:, 0:gw], AF.Exp, scale=-0.5, inc=("sm", 1))
    k.tt("V", nmr[:, 0:gw], m[:, 0:gw], rstd[:, 0:gw], ALU.mult,
         waits=[("sm", k.cnt("sm"))])
    k.ts("V", nmr[:, 0:gw], nmr[:, 0:gw], -1.0, None, ALU.mult, inc=("vm", 1))
    vmv = k.cnt("vm")
    k.mm(ln.ps_b[:, 0:gw], ln.onerow_f[0:1, :], rstd[:, 0:gw], start=True, stop=True,
         waits=[("vm", vmv)])
    k.mm(ln.ps_b[:, 512:512 + gw], ln.onerow_f[0:1, :], nmr[:, 0:gw],
         start=True, stop=True, inc=("tm", 1))
    tmv2 = k.cnt("tm")
    k.copy("V", ln.rb[:, 0:gw], ln.ps_b[:, 0:gw], waits=[("tm", tmv2)])
    k.copy("V", ln.nb[:, 0:gw], ln.ps_b[:, 512:512 + gw])
    for c in range(DC):
        k.tt("V", ln.lnt[:, 0:gw], xc(c), ln.rb[:, 0:gw], ALU.mult)
        if gamma is None:
            k.tt("V", out_c(c), ln.lnt[:, 0:gw], ln.nb[:, 0:gw], ALU.add,
                 inc=("vm", 1) if c == DC - 1 else None)
        else:
            k.tt("V", ln.lnt[:, 0:gw], ln.lnt[:, 0:gw], ln.nb[:, 0:gw], ALU.add)
            k.ts("V", out_c(c), ln.lnt[:, 0:gw], gamma(c), beta(c), ALU.mult, ALU.add,
                 inc=("vm", 1) if c == DC - 1 else None)
    return k.cnt("vm")



def pct(dram, t=None):
    return dram.rearrange("(c p) t -> p c t", p=128)


def sl3(sbuf_ap, w):
    return sbuf_ap.rearrange("p (c t) -> p c t", t=w)



def build_rwkv(dump=()):
    k = K("rwkv")
    x = k.din("x", [D, XCOLS])
    for nm in ("wkT", "wvT", "wrT", "woT", "cwkT", "cwvT", "cwrT"):
        k.din(nm, [D, D])
    for nm in ("ln1w", "ln1b", "ln2w", "ln2b", "u", "w", "mk", "mv", "mr",
               "cmk", "cmr", "bo"):
        k.din(nm, [D])
    k.din("c_ones", [128])
    k.din("gatesb", [128, NPASS])
    xo = k.dout("xo", [D, TOWN])

    def scratch(nm, shape, dt):
        return k.dout(nm, shape, dt) if nm in dump else k.dtmp(nm, shape, dt)

    if "att_dbg" in dump:
        k.dout("att_dbg", [D, 1025], BF16)
    if "z2_dbg" in dump:
        k.dout("z2_dbg", [D, 1025], BF16)
    if "kk_dbg" in dump:
        k.dout("kk_dbg", [D, PASS], BF16)
    if "st_dbg" in dump:
        k.dout("st_dbg", [D, PASS], BF16)
    if "y1_dbg" in dump:
        k.dout("y1_dbg", [D, 1025], BF16)
    y0d = scratch("y0d", [D, EXT], BF16)
    z1d = scratch("z1d", [D, EXT], BF16)
    xkd = scratch("xkd", [D, EXT], BF16)
    xvd = scratch("xvd", [D, EXT], BF16)
    sxd = scratch("sxd", [D, EXT], BF16)

    cols = {}
    for nm in ("ln1w", "ln1b", "ln2w", "ln2b", "u", "w", "mk", "mv", "mr",
               "cmk", "cmr", "bo"):
        cols[nm] = k.sb("col_" + nm, 128, DC, F32)
        v = k.dram[nm].rearrange("(c p o) -> c p o", p=128, o=1)
        for c in range(DC):
            k.dma("G", cols[nm][:, c:c + 1], v[c])
    dcols = {}
    for nm in ("mk", "mv", "mr", "cmk", "cmr"):
        dcols[nm] = k.sb("col1m_" + nm, 128, DC, F32)
    dcols["wneg"] = k.sb("col_wneg", 128, DC, F32)
    k.fence_tag("V", "gd")
    for nm in ("mk", "mv", "mr", "cmk", "cmr"):
        k.ts("V", dcols[nm][:, :], cols[nm][:, :], -1.0, 1.0, ALU.mult, ALU.add)
    k.ts("V", dcols["wneg"][:, :], cols["w"][:, :], -1.0, None, ALU.mult, inc=("vm", 1))
    gcol = k.sb("col_gate", 128, NPASS, F32)
    k.dma("G", gcol[:, :], k.dram["gatesb"])

    ln = LNKit(k)
    NW = EXT - 1
    O1 = 1025

    G5 = [(0, 256), (256, 256), (512, 256), (768, 256), (1024, 161)]
    GM = [(1, 256), (257, 256), (513, 256), (769, 256), (1025, 160)]
    GR = [(160, 256), (416, 256), (672, 256), (928, 256), (1184, 1)]
    GC = [(1, 256), (257, 256), (513, 256), (769, 256)]

    S1 = k.sb("poolS1", 128, DC * (GW + 1), BF16)
    S2 = k.sb("poolS2", 128, DC * (GW + 1), BF16)
    S3 = k.sb("poolS3", 128, DC * (GW + 1), BF16)
    wqA = k.sb("wqA", 128, DC * GW, BF16)
    wqB = k.sb("wqB", 128, DC * GW, BF16)
    kq = k.sb("kq", 128, 2 * NW, BF16)
    vq = k.sb("vq", 128, 2 * NW, BF16)
    poolC = k.sb("poolC", 128, DC * O1, BF16)
    poolD = k.sb("poolD", 128, DC * O1, BF16)
    poolK = k.sb("poolK", 128, DC * PASS, BF16)
    mtmp = k.sb("mtmp", 128, GW, BF16)
    evt = k.sb("evt", 128, 2 * GW, F32)
    xf = k.sb("xf", 128, 2 * GW, F32)
    s1t = k.sb("s1t", 128, NW, F32)
    qt = k.sb("qt", 128, NW, F32)
    e1t = k.sb("e1t", 128, NW, BF16)
    e2t = k.sb("e2t", 128, NW, BF16)
    evscan = k.sb("evscan", 128, NW, BF16)
    aT = k.sb("aT", 128, NW, F32)
    bT = k.sb("bT", 128, NW, F32)
    agate = k.sb("agate", 128, 2, F32)
    ps_sw = k.ps("ps_sw", 128, 4 * 512, F32)

    pxe = pct(x)

    def slab(pool):
        return sl3(pool, GW + 1)

    def wslab(wbuf):
        return sl3(wbuf, GW)

    def sweep_ocg(wname, wbuf, ocg, groups, src_d, src_sh, evac, rhs_sb=None, rhs_w=0):
        wd = pct(k.dram[wname])
        o0 = ocg * 2 * 128
        wtok = k.dma("G", wslab(wbuf)[:, :, 0:GW], wd[:, :, o0:o0 + 2 * 128])
        for (g0, gw) in groups:
            stok = None
            if src_d is not None:
                k.hard()
                stok = k.dma("G", slab(S1)[:, :, 0:gw],
                             pct(src_d)[:, :, g0 + src_sh:g0 + src_sh + gw],
                             slow=(gw < 16))
            for ocp in range(2):
                oc = ocg * 2 + ocp
                psl = ps_sw[:, ocp * 512:ocp * 512 + gw]
                for kc in range(DC):
                    rhs = (slab(S1)[:, kc, 0:gw] if src_d is not None
                           else rhs_sb(kc, g0, gw))
                    k.mm(psl, wslab(wbuf)[:, kc, ocp * 128:ocp * 128 + 128], rhs,
                         start=(kc == 0), stop=(kc == DC - 1),
                         waits=(([wtok] + ([stok] if stok else [])) if kc == 0 else None),
                         inc=("tm", 1) if kc == DC - 1 else None)
                evac(oc, ocp, g0, gw, psl)

    for p in range(NPASS):
        w0 = p * PASS
        gate_col = gcol[:, p:p + 1]

        for (g0, gw) in G5:
            k.hard()
            xtok = k.dma("G", slab(S1)[:, :, 0:gw], pxe[:, :, w0 + g0:w0 + g0 + gw])
            k.wait("T", *xtok)
            k.wait("S", *xtok)
            emit_ln(k, ln, lambda c: slab(S1)[:, c, 0:gw], gw,
                    lambda c: slab(S2)[:, c, 0:gw])
            emit_ln(k, ln, lambda c: slab(S2)[:, c, 0:gw], gw,
                    lambda c: slab(S3)[:, c, 0:gw],
                    gamma=lambda c: cols["ln1w"][:, c:c + 1],
                    beta=lambda c: cols["ln1b"][:, c:c + 1])
            if g0 <= 160 < g0 + gw:
                for c in range(DC):
                    k.tt("V", slab(S3)[:, c, 160 - g0:161 - g0],
                         slab(S3)[:, c, 160 - g0:161 - g0], gate_col, ALU.mult,
                         inc=("vm", 1) if c == DC - 1 else None)
            vmv = k.cnt("vm")
            k.wait("Y", "vm", vmv)
            k.dma("Y", pct(y0d)[:, :, g0:g0 + gw], slab(S2)[:, :, 0:gw])
            k.dma("Y", pct(z1d)[:, :, g0:g0 + gw], slab(S3)[:, :, 0:gw])

        for (g0, gw) in GM:
            k.hard()
            mtok = k.dma("G", slab(S1)[:, :, 0:gw + 1], pct(z1d)[:, :, g0 - 1:g0 + gw])
            k.wait("V", *mtok)
            for (mixn, dstd) in (("mk", xkd), ("mv", xvd), ("mr", sxd)):
                for c in range(DC):
                    zt = slab(S1)[:, c, 1:gw + 1]
                    zs = slab(S1)[:, c, 0:gw]
                    k.ts("V", mtmp[:, 0:gw], zs, dcols[mixn][:, c:c + 1], None, ALU.mult)
                    k.stt("V", slab(S2)[:, c, 0:gw], zt, cols[mixn][:, c:c + 1],
                          mtmp[:, 0:gw], ALU.mult, ALU.add,
                          inc=("vm", 1) if c == DC - 1 else None)
                vmv = k.cnt("vm")
                if mixn == "mr":
                    k.fence("S", "vm")
                    for c in range(DC):
                        k.act(slab(S2)[:, c, 0:gw], slab(S2)[:, c, 0:gw], AF.Sigmoid,
                              inc=("sm", 1) if c == DC - 1 else None)
                    k.wait("Y", "sm", k.cnt("sm"))
                else:
                    k.wait("Y", "vm", vmv)
                k.dma("Y", pct(dstd)[:, :, g0:g0 + gw], slab(S2)[:, :, 0:gw])
                k.fence_tag("V", "yd")

        for ocg in range(8):
            k.hard()
            def evac_r(oc, ocp, g0, gw, psl):
                k.act(sl3(poolC, O1)[:, oc, g0 - 160:g0 - 160 + gw], psl, AF.Identity,
                      waits=[("tm", k.cnt("tm"))], inc=("sm", 1))
            sweep_ocg("wrT", wqA, ocg, GR, sxd, 0, evac_r)
            k.hard()
            def evac_k(oc, ocp, g0, gw, psl):
                k.copy("V", sl3(kq, NW)[:, ocp, g0 - 1:g0 - 1 + gw], psl,
                       waits=[("tm", k.cnt("tm"))], inc=("vm", 1))
            def evac_v(oc, ocp, g0, gw, psl):
                k.copy("V", sl3(vq, NW)[:, ocp, g0 - 1:g0 - 1 + gw], psl,
                       waits=[("tm", k.cnt("tm"))], inc=("vm", 1))
            sweep_ocg("wkT", wqB, ocg, GM, xkd, 0, evac_k)
            k.hard()
            sweep_ocg("wvT", wqA, ocg, GM, xvd, 0, evac_v)
            k.hard()
            for ocp in range(2):
                oc = ocg * 2 + ocp
                kt = sl3(kq, NW)[:, ocp, :]
                vt = sl3(vq, NW)[:, ocp, :]
                k.fence("V", "sm")
                k.ts("V", s1t[:, :], kt, cols["u"][:, oc:oc + 1], None, ALU.add)
                k.ts("V", qt[:, :], s1t[:, :], cols["w"][:, oc:oc + 1], None, ALU.max)
                k.tt("V", s1t[:, :], s1t[:, :], qt[:, :], ALU.subtract, inc=("vm", 1))
                vmv = k.cnt("vm")
                k.act(e1t[:, :], qt[:, :], AF.Exp, bias=dcols["wneg"][:, oc:oc + 1],
                      scale=-1.0, waits=[("vm", vmv)])
                k.act(e2t[:, :], s1t[:, :], AF.Exp, inc=("sm", 1))
                smv = k.cnt("sm")
                k.tt("V", evscan[:, :], e2t[:, :], vt, ALU.mult, waits=[("sm", smv)])
                k.scan(aT[:, 0:159], e1t[:, 0:159], evscan[:, 0:159], 0.0,
                       ALU.mult, ALU.add)
                k.scan(bT[:, 0:159], e1t[:, 0:159], e2t[:, 0:159], 0.0,
                       ALU.mult, ALU.add)
                k.tt("V", agate[:, 0:1], aT[:, 158:159], gate_col, ALU.mult)
                k.tt("V", agate[:, 1:2], bT[:, 158:159], gate_col, ALU.mult)
                k.scan(aT[:, 159:NW], e1t[:, 159:NW], evscan[:, 159:NW],
                       agate[:, 0:1], ALU.mult, ALU.add)
                k.scan(bT[:, 159:NW], e1t[:, 159:NW], e2t[:, 159:NW],
                       agate[:, 1:2], ALU.mult, ALU.add, inc=("vm", 1))
                vmv = k.cnt("vm")
                k.act(qt[:, 0:O1], bT[:, 159:NW], AF.Ln, waits=[("vm", vmv)])
                k.act(qt[:, 0:O1], qt[:, 0:O1], AF.Exp, scale=-1.0, inc=("sm", 1))
                smv = k.cnt("sm")
                k.tt("V", s1t[:, 0:O1], aT[:, 159:NW], qt[:, 0:O1], ALU.mult,
                     waits=[("sm", smv)])
                rt = sl3(poolC, O1)[:, oc, :]
                k.tt("V", rt, s1t[:, 0:O1], rt, ALU.mult, inc=("vm", 1))

        k.hard()
        if "att_dbg" in dump:
            for _c in range(DC):
                k.dma("Y", chunks(k.dram["att_dbg"])[_c], sl3(poolC, O1)[:, _c, :])
            k.fence_tag("T", "yd")
        for ocg in range(8):
            k.hard()
            wd = pct(k.dram["woT"])
            o0 = ocg * 256
            wtok = k.dma("G", wslab(wqA)[:, :, 0:GW], wd[:, :, o0:o0 + 256])
            for (g0, gw) in GR:
                k.hard()
                stok = k.dma("G", slab(S1)[:, :, 0:gw], pct(y0d)[:, :, g0:g0 + gw],
                             slow=(gw < 16))
                for ocp in range(2):
                    oc = ocg * 2 + ocp
                    psl = ps_sw[:, ocp * 512:ocp * 512 + gw]
                    for kc in range(DC):
                        k.mm(psl, wslab(wqA)[:, kc, ocp * 128:ocp * 128 + 128],
                             sl3(poolC, O1)[:, kc, g0 - 160:g0 - 160 + gw],
                             start=(kc == 0), stop=(kc == DC - 1),
                             waits=([wtok, stok] if kc == 0 else None),
                             inc=("tm", 1) if kc == DC - 1 else None)
                    k.act(evt[:, ocp * GW:ocp * GW + gw], psl, AF.Identity,
                          bias=cols["bo"][:, oc:oc + 1],
                          waits=[("tm", k.cnt("tm")), ("vm", k.cnt("vm"))],
                          inc=("sm", 1))
                    k.tt("V", sl3(poolD, O1)[:, oc, g0 - 160:g0 - 160 + gw],
                         evt[:, ocp * GW:ocp * GW + gw], slab(S1)[:, oc, 0:gw],
                         ALU.add, waits=[("sm", k.cnt("sm"))], inc=("vm", 1))

        if "y1_dbg" in dump:
            k.hard()
            for _c in range(DC):
                k.dma("Y", chunks(k.dram["y1_dbg"])[_c], sl3(poolD, O1)[:, _c, :])
            k.fence_tag("V", "yd")
        for (g0, gw) in ((0, 256), (256, 256), (512, 256), (768, 256), (1024, 1)):
            k.hard()
            emit_ln(k, ln, lambda c: sl3(poolD, O1)[:, c, g0:g0 + gw], gw,
                    lambda c: sl3(poolC, O1)[:, c, g0:g0 + gw],
                    gamma=lambda c: cols["ln2w"][:, c:c + 1],
                    beta=lambda c: cols["ln2b"][:, c:c + 1])
        for c in range(DC):
            k.tt("V", sl3(poolC, O1)[:, c, 0:1], sl3(poolC, O1)[:, c, 0:1],
                 gate_col, ALU.mult, inc=("vm", 1) if c == DC - 1 else None)

        if "z2_dbg" in dump:
            k.hard()
            for _c in range(DC):
                k.dma("Y", chunks(k.dram["z2_dbg"])[_c], sl3(poolC, O1)[:, _c, :])
            k.fence_tag("V", "yd")
        for (g0, gw) in GC:
            k.hard()
            for c in range(DC):
                zt = sl3(poolC, O1)[:, c, g0:g0 + gw]
                zs = sl3(poolC, O1)[:, c, g0 - 1:g0 - 1 + gw]
                k.ts("V", mtmp[:, 0:gw], zs, dcols["cmk"][:, c:c + 1], None, ALU.mult)
                k.stt("V", slab(S2)[:, c, 0:gw], zt, cols["cmk"][:, c:c + 1],
                      mtmp[:, 0:gw], ALU.mult, ALU.add,
                      inc=("vm", 1) if c == DC - 1 else None)
            stv = k.cnt("vm")
            if "st_dbg" in dump:
                k.hard()
                for _c in range(DC):
                    k.dma("Y", chunks(k.dram["st_dbg"])[_c][:, g0 - 1:g0 - 1 + gw],
                          slab(S2)[:, _c, 0:gw], )
                k.fence_tag("T", "yd")
            for ocg in range(8):
                if ocg:
                    k.hard()
                wd = pct(k.dram["cwkT"])
                o0 = ocg * 256
                wtok = k.dma("G", wslab(wqA)[:, :, 0:GW], wd[:, :, o0:o0 + 256])
                for ocp in range(2):
                    oc = ocg * 2 + ocp
                    psl = ps_sw[:, ocp * 512:ocp * 512 + gw]
                    for kc in range(DC):
                        k.mm(psl, wslab(wqA)[:, kc, ocp * 128:ocp * 128 + 128],
                             slab(S2)[:, kc, 0:gw],
                             start=(kc == 0), stop=(kc == DC - 1),
                             waits=([wtok, ("vm", stv)] if kc == 0 else None),
                             inc=("tm", 1) if kc == DC - 1 else None)
                    k.act(mtmp[:, 0:gw], psl, AF.Relu,
                          waits=[("tm", k.cnt("tm")), ("vm", k.cnt("vm"))],
                          inc=("sm", 1))
                    k.tt("V", sl3(poolK, PASS)[:, oc, g0 - 1:g0 - 1 + gw],
                         mtmp[:, 0:gw], mtmp[:, 0:gw], ALU.mult,
                         waits=[("sm", k.cnt("sm"))], inc=("vm", 1))

        if "kk_dbg" in dump:
            k.hard()
            for _c in range(DC):
                k.dma("Y", chunks(k.dram["kk_dbg"])[_c], sl3(poolK, PASS)[:, _c, :])
            k.fence_tag("V", "yd")
        for (g0, gw) in GC:
            k.hard()
            for c in range(DC):
                zt = sl3(poolC, O1)[:, c, g0:g0 + gw]
                zs = sl3(poolC, O1)[:, c, g0 - 1:g0 - 1 + gw]
                k.ts("V", mtmp[:, 0:gw], zs, dcols["cmr"][:, c:c + 1], None, ALU.mult)
                k.stt("V", slab(S2)[:, c, 0:gw], zt, cols["cmr"][:, c:c + 1],
                      mtmp[:, 0:gw], ALU.mult, ALU.add,
                      inc=("vm", 1) if c == DC - 1 else None)
            stv = k.cnt("vm")
            for ocg in range(8):
                k.hard()
                o0 = ocg * 256
                wtokA = k.dma("G", wslab(wqA)[:, :, 0:GW],
                              pct(k.dram["cwrT"])[:, :, o0:o0 + 256])
                wtokB = k.dma("G", wslab(wqB)[:, :, 0:GW],
                              pct(k.dram["cwvT"])[:, :, o0:o0 + 256])
                for ocp in range(2):
                    oc = ocg * 2 + ocp
                    xtok = k.dma("Y", xf[:, ocp * GW:ocp * GW + gw],
                                 pxe[:, oc, w0 + g0 + 160:w0 + g0 + 160 + gw])
                    pr = ps_sw[:, ocp * 512:ocp * 512 + gw]
                    pv = ps_sw[:, (2 + ocp) * 512:(2 + ocp) * 512 + gw]
                    for kc in range(DC):
                        k.mm(pr, wslab(wqA)[:, kc, ocp * 128:ocp * 128 + 128],
                             slab(S2)[:, kc, 0:gw],
                             start=(kc == 0), stop=(kc == DC - 1),
                             waits=([wtokA, wtokB, ("vm", stv)] if kc == 0 else None))
                    for kc in range(DC):
                        k.mm(pv, wslab(wqB)[:, kc, ocp * 128:ocp * 128 + 128],
                             sl3(poolK, PASS)[:, kc, g0 - 1:g0 - 1 + gw],
                             start=(kc == 0), stop=(kc == DC - 1),
                             inc=("tm", 1) if kc == DC - 1 else None)
                    k.act(mtmp[:, 0:gw], pr, AF.Sigmoid,
                          waits=[("tm", k.cnt("tm")), ("vm", k.cnt("vm"))],
                          inc=("sm", 1))
                    k.tt("V", evt[:, ocp * GW:ocp * GW + gw], pv, mtmp[:, 0:gw],
                         ALU.mult, waits=[("sm", k.cnt("sm")), xtok])
                    k.tt("V", evt[:, ocp * GW:ocp * GW + gw],
                         evt[:, ocp * GW:ocp * GW + gw],
                         sl3(poolD, O1)[:, oc, g0:g0 + gw], ALU.add)
                    k.tt("V", xf[:, ocp * GW:ocp * GW + gw],
                         evt[:, ocp * GW:ocp * GW + gw], xf[:, ocp * GW:ocp * GW + gw],
                         ALU.add, inc=("vm", 1))
                    k.wait("Y", "vm", k.cnt("vm"))
                    k.dma("Y", pct(xo)[:, oc, w0 + g0 - 1:w0 + g0 - 1 + gw],
                          xf[:, ocp * GW:ocp * GW + gw])

    k.hard()
    return k



def _t(a):
    return np.ascontiguousarray(np.asarray(a).T.astype(np.float32))


def _f(a):
    return np.ascontiguousarray(np.asarray(a).astype(np.float32))


def prep_rwkv(inputs, xfull):
    base = {
        "wkT": _t(inputs["L0_attn_a_tm_Wk"]), "wvT": _t(inputs["L0_attn_a_tm_Wv"]),
        "wrT": _t(inputs["L0_attn_a_tm_Wr"]), "woT": _t(inputs["L0_attn_a_tm_Wo"]),
        "cwkT": _t(inputs["L0_attn_a_cm_Wk"]), "cwvT": _t(inputs["L0_attn_a_cm_Wv"]),
        "cwrT": _t(inputs["L0_attn_a_cm_Wr"]),
        "ln1w": _f(inputs["L0_attn_a_ln1_w"]), "ln1b": _f(inputs["L0_attn_a_ln1_b"]),
        "ln2w": _f(inputs["L0_attn_a_ln2_w"]), "ln2b": _f(inputs["L0_attn_a_ln2_b"]),
        "u": _f(inputs["L0_attn_a_tm_u"]), "w": _f(inputs["L0_attn_a_tm_w"]),
        "mk": _f(inputs["L0_attn_a_tm_mix_k"]), "mv": _f(inputs["L0_attn_a_tm_mix_v"]),
        "mr": _f(inputs["L0_attn_a_tm_mix_r"]),
        "cmk": _f(inputs["L0_attn_a_cm_mix_k"]), "cmr": _f(inputs["L0_attn_a_cm_mix_r"]),
        "bo": _f(inputs["L0_attn_a_tm_bo"]),
        "c_ones": np.ones(128, np.float32),
    }
    maps = []
    for core in range(NCORES):
        b, h = core // 2, core % 2
        tok0 = h * TOWN
        ext = np.zeros((XCOLS, D), np.float32)
        lo = tok0 - (LB + 1)
        if lo < 0:
            ext[LB + 1:] = xfull[b, 0:TOWN]
        else:
            ext[:] = xfull[b, lo:tok0 + TOWN]
        gates = np.array([0.0, 1.0], np.float32) if h == 0 else np.array([1.0, 1.0], np.float32)
        m = dict(base)
        m["x"] = np.ascontiguousarray(ext.T)
        m["gatesb"] = np.ascontiguousarray(np.tile(gates, (128, 1)).astype(np.float32))
        maps.append(m)
    return maps


def gather_fm(res, key="xo"):
    out = np.empty((B, T, D), np.float32)
    for core in range(NCORES):
        b, h = core // 2, core % 2
        out[b, h * TOWN:(h + 1) * TOWN] = np.asarray(res.results[core][key]).T
    return out



def ln_prologue(k, ln, xdram, xh, n):
    pxd = pct(xdram)
    for g0 in range(0, n, GW):
        gw = min(GW, n - g0)
        tok = k.dma("G", sl3(xh, n)[:, :, g0:g0 + gw], pxd[:, :, g0:g0 + gw])
        k.wait("T", *tok)
        k.wait("S", *tok)
        k.wait("V", *tok)
        emit_ln(k, ln, lambda c: sl3(xh, n)[:, c, g0:g0 + gw], gw,
                lambda c: sl3(xh, n)[:, c, g0:g0 + gw])



def build_moe8(dump=()):
    k = K("moe8")
    E8, HE = 8, 1024
    xin = k.din("x", [D, TOWN])
    k.din("wg1T", [D, E8])
    k.din("wg2T", [E8, E8])
    k.din("wupT", [E8, D, HE])
    k.din("wdnT", [E8 * HE, D])
    k.din("bup", [E8, HE])
    k.din("bdn", [E8, D])
    k.din("c_ones", [128])
    k.din("c_eyef", [128, 128])
    xo = k.dout("xo", [D, TOWN])
    hd = k.dtmp("hd", [E8 * HE, TOWN], BF16)
    wdnb = k.dtmp("wdnb", [E8 * HE, D], BF16)

    ln = LNKit(k)
    onerow_bf = k.sb("onerow_bf", 1, 128, BF16)
    k.dma("G", onerow_bf[0:1, :], k.dram["c_ones"].rearrange("(o p) -> o p", o=1, p=128))
    xh = k.sb("xh", 128, DC * TOWN, BF16)
    eyef = k.sb("eyef", 128, 128, F32)
    k.dma("G", eyef[:, :], k.dram["c_eyef"])
    ps_bc = k.ps("ps_bc", 128, 512, F32)
    ps_sw = k.ps("ps_sw", 128, 4 * 512, F32)

    ln_prologue(k, ln, xin, xh, TOWN)

    wcv = k.sb("wcv", 128, 2 * 2048, BF16)
    pw = k.dram["wdnT"].rearrange("(r p) d -> r p d", p=128)
    pwb = wdnb.rearrange("(r p) d -> r p d", p=128)
    st_tok = [None, None]
    for r in range(E8 * HE // 128):
        pp = r % 2
        wv = wcv[:, pp * 2048:(pp + 1) * 2048]
        lt = k.dma("G", wv, pw[r], waits=[st_tok[pp]] if st_tok[pp] else None)
        st_tok[pp] = k.dma("Y", pwb[r], wv, waits=[lt])

    wg1 = k.sb("wg1", 128, DC * E8, BF16)
    k.dma("G", sl3(wg1, E8), pct(k.dram["wg1T"]))
    wg2 = k.sb("wg2", 8, E8, BF16)
    tokg = k.dma("G", wg2[:, :], k.dram["wg2T"])
    l1 = k.sb("l1", 8, TOWN, BF16)
    l2f = k.sb("l2f", 8, TOWN, F32)
    lgt = k.sb("lgt", 128, 16 * E8, F32)
    wet = k.sb("wet", 128, 16 * E8, F32)
    wef = k.sb("wef", 8, TOWN, BF16)
    bdnb = k.sb("bdnb", 8, D, BF16)
    k.dma("G", bdnb[:, :], k.dram["bdn"])
    gsc = {nm: k.sb("g_" + nm, 128, 1, F32) for nm in ("m1", "nm1", "m2", "s", "inv")}
    gt1 = k.sb("g_t1", 128, E8, F32)
    k.fence_tag("T", "gd")
    k.fence_tag("V", "gd")
    k.fence("T", "vm")
    for g0 in range(0, TOWN, GW):
        gw = GW
        for c in range(DC):
            k.mm(ps_bc[0:8, 0:gw], sl3(wg1, E8)[:, c, :], sl3(xh, TOWN)[:, c, g0:g0 + gw],
                 start=(c == 0), stop=(c == DC - 1),
                 waits=[("sm", k.cnt("sm"))] if c == 0 else None,
                 inc=("tm", 1) if c == DC - 1 else None)
        k.act(l1[:, g0:g0 + gw], ps_bc[0:8, 0:gw], AF.Tanh, waits=[("tm", k.cnt("tm"))])
    k.fence("T", "sm")
    for g0 in range(0, TOWN, GW):
        gw = GW
        k.mm(ps_bc[0:8, 0:gw], wg2[:, :], l1[:, g0:g0 + gw], start=True, stop=True,
             waits=([tokg, ("vm", k.cnt("vm"))] if g0 == 0 else [("vm", k.cnt("vm"))]),
             inc=("tm", 1))
        k.copy("V", l2f[:, g0:g0 + gw], ps_bc[0:8, 0:gw], waits=[("tm", k.cnt("tm"))])
    k.fence("T", "vm")
    for ti in range(16):
        pb = ln.ps_b[:, (ti % 2) * 512:(ti % 2) * 512 + 8]
        k.transp(pb, l2f[:, ti * 128:(ti + 1) * 128], eyef[0:8, 0:8],
                 waits=[("vm", k.cnt("vm"))], inc=("tm", 1))
        k.copy("V", lgt[:, ti * E8:(ti + 1) * E8], pb,
               waits=[("tm", k.cnt("tm"))])
    for ti in range(16):
        lg = lgt[:, ti * E8:(ti + 1) * E8]
        we = wet[:, ti * E8:(ti + 1) * E8]
        k.red("V", gsc["m1"][:, 0:1], lg, ALU.max)
        k.ts("V", gsc["nm1"][:, 0:1], gsc["m1"][:, 0:1], -1.0, None, ALU.mult)
        k.ts("V", gt1[:, :], lg, gsc["m1"][:, 0:1], -1e30, ALU.is_ge, ALU.mult)
        k.tt("V", gt1[:, :], lg, gt1[:, :], ALU.add)
        k.red("V", gsc["m2"][:, 0:1], gt1[:, :], ALU.max, inc=("vm", 1))
        k.act(we, lg, AF.Exp, bias=gsc["nm1"][:, 0:1], waits=[("vm", k.cnt("vm"))],
              inc=("sm", 1))
        k.ts("V", gt1[:, :], lg, gsc["m2"][:, 0:1], None, ALU.is_ge,
             waits=[("sm", k.cnt("sm"))])
        k.tt("V", we, we, gt1[:, :], ALU.mult)
        k.red("V", gsc["s"][:, 0:1], we, ALU.add)
        k.recip(gsc["inv"][:, 0:1], gsc["s"][:, 0:1])
        k.ts("V", we, we, gsc["inv"][:, 0:1], None, ALU.mult, inc=("vm", 1))
    k.fence("T", "vm")
    for ti in range(16):
        pb = ln.ps_b[:, (ti % 2) * 512:(ti % 2) * 512 + 128]
        k.transp(pb[0:8, 0:128], wet[:, ti * E8:(ti + 1) * E8], eyef[:, :],
                 waits=[("vm", k.cnt("vm"))], inc=("tm", 1))
        k.copy("V", wef[:, ti * 128:(ti + 1) * 128], pb[0:8, 0:128],
               waits=[("tm", k.cnt("tm"))])

    bupc = k.sb("bupc", 128, E8 * E8, F32)
    bv = k.dram["bup"].rearrange("e (c p) -> e c p", p=128)
    for e in range(E8):
        for c in range(E8):
            k.dma("G", bupc[:, e * E8 + c:e * E8 + c + 1],
                  bv[e, c].rearrange("(p o) -> p o", p=128, o=1))
    wq = k.sb("wq", 128, DC * 512, BF16)
    hsl = k.sb("hsl", 128, 2 * 512, BF16)
    wrow = k.sb("wrow", 1, TOWN, BF16)
    phd = hd.rearrange("(e c p) t -> e p c t", p=128, c=E8)
    k.hard()
    hst = [None, None]
    for e in range(E8):
        rtok = k.dma("G", wrow[0:1, :], wef[e:e + 1, :], waits=[("vm", k.cnt("vm"))])
        for oh in range(2):
            k.hard()
            wtok = k.dma("G", sl3(wq, 512),
                         k.dram["wupT"][e].rearrange("(c p) o -> p c o", p=128)[:, :, oh * 512:(oh + 1) * 512])
            for tg in range(4):
                t0 = tg * 512
                bcv = k.cnt("vm")
                k.mm(ps_bc[:, 0:512], onerow_bf[0:1, :], wef[0:1, t0:t0 + 512],
                     start=True, stop=True, waits=[("vm", bcv)], inc=("tm", 1))                     if False else None
                k.mm(ps_bc[:, 0:512], onerow_bf[0:1, :], wrow[0:1, t0:t0 + 512],
                     start=True, stop=True, waits=[("vm", bcv), rtok], inc=("tm", 1))
                bct = k.cnt("tm")
                for hc4 in range(4):
                    hc = oh * 4 + hc4
                    pp = hc4 % 2
                    psl = ps_sw[:, pp * 512:pp * 512 + 512]
                    for kc in range(DC):
                        k.mm(psl, sl3(wq, 512)[:, kc, hc4 * 128:hc4 * 128 + 128],
                             sl3(xh, TOWN)[:, kc, t0:t0 + 512],
                             start=(kc == 0), stop=(kc == DC - 1),
                             waits=([wtok, ("sm", k.cnt("sm"))] if kc == 0 else None),
                             inc=("tm", 1) if kc == DC - 1 else None)
                    k.act(hsl[:, pp * 512:pp * 512 + 512], psl, AF.Silu,
                          bias=bupc[:, e * E8 + hc:e * E8 + hc + 1],
                          waits=[("tm", k.cnt("tm")), ("vm", k.cnt("vm"))] +
                                ([hst[pp]] if hst[pp] else []),
                          inc=("sm", 1))
                    k.tt("V", hsl[:, pp * 512:pp * 512 + 512],
                         hsl[:, pp * 512:pp * 512 + 512], ps_bc[:, 0:512], ALU.mult,
                         waits=[("sm", k.cnt("sm")), ("tm", bct)], inc=("vm", 1))
                    hst[pp] = k.dma("Y", phd[e][:, hc:hc + 1, t0:t0 + 512],
                                    hsl[:, pp * 512:pp * 512 + 512],
                                    waits=[("vm", k.cnt("vm"))])
    k.hard()
    Hbuf = k.sb("Hbuf", 128, 64 * 256, BF16)
    wd2 = k.sb("wd2", 128, 2 * 64 * 128, BF16)
    xfb = k.sb("xfb", 128, 2 * 256, F32)
    ob = k.sb("ob", 128, 2 * 256, F32)
    phr = hd.rearrange("(r p) t -> p r t", p=128)
    pwr = wdnb.rearrange("(r p) d -> p r d", p=128)
    for tg in range(8):
        t0 = tg * 256
        k.hard()
        htok1 = k.dma("G", Hbuf.rearrange("p (r t) -> p r t", t=256)[:, 0:32, :],
                      phr[:, 0:32, t0:t0 + 256])
        htok2 = k.dma("G", Hbuf.rearrange("p (r t) -> p r t", t=256)[:, 32:64, :],
                      phr[:, 32:64, t0:t0 + 256])
        wtok = [None, None]
        evv = [None, None]
        otok = [None, None]
        for ocd in range(DC):
            pp = ocd % 2
            wv = wd2.rearrange("p (b r o) -> p b r o", b=2, o=128)[:, pp]
            wtok[pp] = k.dma("G", wv, pwr[:, :, ocd * 128:(ocd + 1) * 128],
                             waits=[("tm", k.cnt("tm"))])
            xtok = k.dma("Y", xfb[:, pp * 256:pp * 256 + 256],
                         pct(xin)[:, ocd, t0:t0 + 256])
            psl = ps_sw[:, (2 + pp) * 512:(2 + pp) * 512 + 256]
            for ec in range(64):
                k.mm(psl, wv[:, ec, :],
                     Hbuf.rearrange("p (r t) -> p r t", t=256)[:, ec, :],
                     start=(ec == 0), stop=False,
                     waits=(([htok1, htok2, wtok[pp]] +
                             ([evv[pp]] if evv[pp] else [])) if ec == 0 else None))
            k.mm(psl, bdnb[:, ocd * 128:(ocd + 1) * 128], wef[:, t0:t0 + 256],
                 start=False, stop=True, inc=("tm", 1))
            k.tt("V", ob[:, pp * 256:pp * 256 + 256], psl, xfb[:, pp * 256:pp * 256 + 256],
                 ALU.add, waits=[("tm", k.cnt("tm")), xtok] +
                               ([otok[pp]] if otok[pp] else []),
                 inc=("vm", 1))
            evv[pp] = ("vm", k.cnt("vm"))
            otok[pp] = k.dma("Y", pct(xo)[:, ocd, t0:t0 + 256],
                             ob[:, pp * 256:pp * 256 + 256],
                             waits=[("vm", k.cnt("vm"))])
    k.hard()
    return k



def build_moe0(dump=()):
    k = K("moe0")
    E, I0, RG = 32, 1408, 128
    IC = I0 // 128
    SI = 2816
    SIC = SI // 128
    xin = k.din("x", [D, TOWN])
    k.din("gateT", [D, E])
    k.din("rgT", [D, RG])
    k.din("ruT", [D, RG])
    k.din("rdT", [I0, RG])
    k.din("ugT", [E * RG, RG])
    k.din("cgT", [E * RG, I0])
    k.din("uuT", [E * RG, RG])
    k.din("cuT", [E * RG, I0])
    k.din("udT", [E * RG, RG])
    k.din("cdT", [E * RG, D])
    k.din("s_rgT", [D, RG])
    k.din("s_ugT", [RG, RG])
    k.din("s_cgT", [RG, SI])
    k.din("s_ruT", [D, RG])
    k.din("s_uuT", [RG, RG])
    k.din("s_cuT", [RG, SI])
    k.din("s_rdT", [SI, RG])
    k.din("s_udT", [RG, RG])
    k.din("s_cdT", [RG, D])
    k.din("c_ones", [128])
    k.din("c_eyef", [128, 128])
    xo = k.dout("xo", [D, TOWN])
    h2d = k.dtmp("h2d", [E * RG, TOWN], BF16)
    cdb = k.dtmp("cdb", [E * RG, D], BF16)

    ln = LNKit(k)
    onerow_bf = k.sb("onerow_bf", 1, 128, BF16)
    k.dma("G", onerow_bf[0:1, :], k.dram["c_ones"].rearrange("(o p) -> o p", o=1, p=128))
    xh = k.sb("xh", 128, DC * TOWN, BF16)
    eyef = k.sb("eyef", 128, 128, F32)
    k.dma("G", eyef[:, :], k.dram["c_eyef"])
    ps_bc = k.ps("ps_bc", 128, 512, F32)
    ps_sw = k.ps("ps_sw", 128, 4 * 512, F32)

    ln_prologue(k, ln, xin, xh, TOWN)

    wcv = k.sb("wcv", 128, 2 * 2048, BF16)
    pw = k.dram["cdT"].rearrange("(r p) d -> r p d", p=128)
    pwb = cdb.rearrange("(r p) d -> r p d", p=128)
    st_tok = [None, None]
    for r in range(E * RG // 128):
        pp = r % 2
        wv = wcv[:, pp * 2048:(pp + 1) * 2048]
        lt = k.dma("G", wv, pw[r], waits=[st_tok[pp]] if st_tok[pp] else None)
        st_tok[pp] = k.dma("Y", pwb[r], wv, waits=[lt])

    gwt = k.sb("gwt", 128, DC * E, BF16)
    k.dma("G", sl3(gwt, E), pct(k.dram["gateT"]))
    lgf = k.sb("lgf", E, TOWN, F32)
    lgt = k.sb("lgt", 128, 16 * E, F32)
    wet = k.sb("wet", 128, 16 * E, F32)
    wef = k.sb("wef", E, TOWN, BF16)
    gsc = {nm: k.sb("g_" + nm, 128, 1, F32) for nm in ("m1", "nm1", "mx", "s", "inv")}
    gt1 = k.sb("g_t1", 128, E, F32)
    gt2 = k.sb("g_t2", 128, E, F32)
    k.fence_tag("T", "gd")
    k.fence("T", "vm")
    for g0 in range(0, TOWN, GW):
        gw = GW
        for c in range(DC):
            k.mm(ps_bc[0:E, 0:gw], sl3(gwt, E)[:, c, :], sl3(xh, TOWN)[:, c, g0:g0 + gw],
                 start=(c == 0), stop=(c == DC - 1),
                 waits=[("vm", k.cnt("vm"))] if c == 0 else None,
                 inc=("tm", 1) if c == DC - 1 else None)
        k.copy("V", lgf[:, g0:g0 + gw], ps_bc[0:E, 0:gw], waits=[("tm", k.cnt("tm"))])
    k.fence("T", "vm")
    for ti in range(16):
        pb = ln.ps_b[:, (ti % 2) * 512:(ti % 2) * 512 + E]
        k.transp(pb, lgf[:, ti * 128:(ti + 1) * 128], eyef[0:E, 0:E],
                 waits=[("vm", k.cnt("vm"))], inc=("tm", 1))
        k.copy("V", lgt[:, ti * E:(ti + 1) * E], pb, waits=[("tm", k.cnt("tm"))])
    for ti in range(16):
        lg = lgt[:, ti * E:(ti + 1) * E]
        we = wet[:, ti * E:(ti + 1) * E]
        k.red("V", gsc["m1"][:, 0:1], lg, ALU.max)
        k.ts("V", gsc["nm1"][:, 0:1], gsc["m1"][:, 0:1], -1.0, None, ALU.mult,
             inc=("vm", 1))
        k.act(we, lg, AF.Exp, bias=gsc["nm1"][:, 0:1], waits=[("vm", k.cnt("vm"))],
              inc=("sm", 1))
        k.copy("V", gt1[:, :], we, waits=[("sm", k.cnt("sm"))])
        for it in range(3):
            k.red("V", gsc["mx"][:, 0:1], gt1[:, :], ALU.max)
            k.ts("V", gt2[:, :], gt1[:, :], gsc["mx"][:, 0:1], -1e30, ALU.is_ge, ALU.mult)
            k.tt("V", gt1[:, :], gt1[:, :], gt2[:, :], ALU.add)
        k.red("V", gsc["mx"][:, 0:1], gt1[:, :], ALU.max)
        k.ts("V", gt2[:, :], we, gsc["mx"][:, 0:1], None, ALU.is_ge)
        k.tt("V", we, we, gt2[:, :], ALU.mult)
        k.red("V", gsc["s"][:, 0:1], we, ALU.add)
        k.recip(gsc["inv"][:, 0:1], gsc["s"][:, 0:1])
        k.ts("V", we, we, gsc["inv"][:, 0:1], None, ALU.mult, inc=("vm", 1))
    k.fence("T", "vm")
    for ti in range(16):
        pb = ln.ps_b[:, (ti % 2) * 512:(ti % 2) * 512 + 128]
        k.transp(pb[0:E, 0:128], wet[:, ti * E:(ti + 1) * E], eyef[:, :],
                 waits=[("vm", k.cnt("vm"))], inc=("tm", 1))
        k.copy("V", wef[:, ti * 128:(ti + 1) * 128], pb[0:E, 0:128],
               waits=[("tm", k.cnt("tm"))])

    wq = k.sb("wq", 128, DC * 512, BF16)
    rgf = k.sb("rgf", 128, TOWN, BF16)
    ruf = k.sb("ruf", 128, TOWN, BF16)

    def rank_proj(dst, wname, src3, src_w, n_kc, evac_fn=None):
        k.hard()
        wtok = k.dma("G", sl3(wq, 512)[:, 0:n_kc, 0:RG],
                     pct(k.dram[wname])[:, 0:n_kc, :])
        for tg in range(4):
            t0 = tg * 512
            pp = tg % 2
            psl = ps_sw[:, pp * 512:pp * 512 + 512]
            for kc in range(n_kc):
                k.mm(psl, sl3(wq, 512)[:, kc, 0:RG], src3[:, kc, t0:t0 + 512],
                     start=(kc == 0), stop=(kc == n_kc - 1),
                     waits=([wtok, ("vm", k.cnt("vm")), ("sm", k.cnt("sm"))]
                            if kc == 0 else None),
                     inc=("tm", 1) if kc == n_kc - 1 else None)
            if evac_fn is None:
                k.copy("V", dst[:, t0:t0 + 512], psl, waits=[("tm", k.cnt("tm"))],
                       inc=("vm", 1))
            else:
                evac_fn(dst, t0, psl)

    rank_proj(rgf, "rgT", sl3(xh, TOWN), TOWN, DC)
    rank_proj(ruf, "ruT", sl3(xh, TOWN), TOWN, DC)

    t1f = k.sb("t1f", 128, TOWN, BF16)
    t2f = k.sb("t2f", 128, TOWN, BF16)
    u2f = k.sb("u2f", 128, TOWN, BF16)
    hshf = k.sb("hshf", 128, TOWN, BF16)
    scg = k.sb("scg", 128, SIC * 128, BF16)
    scu = k.sb("scu", 128, SIC * 128, BF16)
    srd = k.sb("srd", 128, SIC * 128, BF16)
    ssm = k.sb("ssm", 128, 3 * 128, BF16)
    prodt = k.sb("prodt", 128, 512, BF16)
    gslt = k.sb("gslt", 128, 512, BF16)
    rank_proj(t1f, "s_rgT", sl3(xh, TOWN), TOWN, DC)
    rank_proj(u2f, "s_ruT", sl3(xh, TOWN), TOWN, DC)
    tok1 = k.dma("G", ssm[:, 0:128], k.dram["s_ugT"])
    tok2 = k.dma("G", ssm[:, 128:256], k.dram["s_uuT"])
    tok3 = k.dma("G", ssm[:, 256:384], k.dram["s_udT"])
    tok4 = k.dma("G", sl3(scg, 128), k.dram["s_cgT"].rearrange("p (c o) -> p c o", o=128))
    tok5 = k.dma("G", sl3(scu, 128), k.dram["s_cuT"].rearrange("p (c o) -> p c o", o=128))
    tok6 = k.dma("G", sl3(srd, 128), pct(k.dram["s_rdT"]))
    k.hard()
    for tg in range(4):
        t0 = tg * 512
        pA = ps_sw[:, 0:512]
        pB = ps_sw[:, 512:1024]
        k.mm(pA, ssm[:, 0:128], t1f[:, t0:t0 + 512], start=True, stop=True,
             waits=[tok1, ("vm", k.cnt("vm"))], inc=("tm", 1))
        k.mm(pB, ssm[:, 128:256], u2f[:, t0:t0 + 512], start=True, stop=True,
             waits=[tok2], inc=("tm", 1))
        k.copy("V", t2f[:, t0:t0 + 512], pA, waits=[("tm", k.cnt("tm"))])
        k.copy("V", u2f[:, t0:t0 + 512], pB, inc=("vm", 1))
    k.hard()
    for tg in range(4):
        t0 = tg * 512
        pG = ps_sw[:, 0:512]
        pU = ps_sw[:, 512:1024]
        pR = ps_sw[:, 1024:1536]
        for oc in range(SIC):
            k.mm(pG, sl3(scg, 128)[:, oc, :], t2f[:, t0:t0 + 512], start=True, stop=True,
                 waits=([tok4, tok5, tok6, ("vm", k.cnt("vm"))] if oc == 0 and tg == 0
                        else [("vm", k.cnt("vm"))]),
                 inc=("tm", 1))
            k.mm(pU, sl3(scu, 128)[:, oc, :], u2f[:, t0:t0 + 512], start=True, stop=True,
                 inc=("tm", 1))
            tmv = k.cnt("tm")
            k.act(gslt[:, :], pG, AF.Silu, waits=[("tm", tmv)], inc=("sm", 1))
            k.tt("V", prodt[:, :], gslt[:, :], pU, ALU.mult,
                 waits=[("sm", k.cnt("sm"))], inc=("vm", 1))
            k.mm(pR, sl3(srd, 128)[:, oc, :], prodt[:, :],
                 start=(oc == 0), stop=(oc == SIC - 1),
                 waits=[("vm", k.cnt("vm"))],
                 inc=("tm", 1) if oc == SIC - 1 else None)
        k.copy("V", t1f[:, t0:t0 + 512], pR, waits=[("tm", k.cnt("tm"))], inc=("vm", 1))
    k.hard()
    for tg in range(4):
        t0 = tg * 512
        pp = tg % 2
        psl = ps_sw[:, pp * 512:pp * 512 + 512]
        k.mm(psl, ssm[:, 256:384], t1f[:, t0:t0 + 512], start=True, stop=True,
             waits=[tok3, ("vm", k.cnt("vm"))], inc=("tm", 1))
        k.copy("V", hshf[:, t0:t0 + 512], psl, waits=[("tm", k.cnt("tm"))], inc=("vm", 1))

    ew = k.sb("ew", 128, 2 * (RG + I0 + RG + I0 + RG), BF16)
    EWW = RG + I0 + RG + I0 + RG
    rdt = k.sb("rdt", 128, IC * 128, BF16)
    tokr = k.dma("G", sl3(rdt, 128), pct(k.dram["rdT"]))
    agf = k.sb("agf", 128, 512, BF16)
    auf = k.sb("auf", 128, 512, BF16)
    rdsb = k.sb("rdsb", 128, 512, BF16)
    wrow = k.sb("wrow", 1, TOWN, BF16)
    h2t = k.sb("h2t", 128, 512, BF16)
    ph2 = h2d.rearrange("(e p) t -> e p t", p=128)
    hprev = [None]
    k.hard()
    for e in range(E):
        pp = e % 2
        ewv = ew[:, pp * EWW:(pp + 1) * EWW]
        offs = [0, RG, RG + I0, RG + I0 + RG, RG + I0 + RG + I0]
        names = ["ugT", "cgT", "uuT", "cuT", "udT"]
        widths = [RG, I0, RG, I0, RG]
        toks = []
        first = True
        for nm, off, wd_ in zip(names, offs, widths):
            tok = k.dma("G", ewv[:, off:off + wd_],
                        k.dram[nm].rearrange("(e p) o -> e p o", p=128)[e],
                        waits=[("tm", k.cnt("tm"))] if first else None)
            first = False
            toks.append(tok)
        rtok = k.dma("G", wrow[0:1, :], wef[e:e + 1, :], waits=[("vm", k.cnt("vm"))])
        for tg in range(4):
            t0 = tg * 512
            pA = ps_sw[:, 0:512]
            pB = ps_sw[:, 512:1024]
            pR = ps_sw[:, 1024:1536]
            pD = ps_sw[:, 1536:2048]
            k.mm(pA, ewv[:, 0:RG], rgf[:, t0:t0 + 512], start=True, stop=True,
                 waits=toks + [("vm", k.cnt("vm")), ("sm", k.cnt("sm"))], inc=("tm", 1))
            k.mm(pB, ewv[:, RG + I0:RG + I0 + RG], ruf[:, t0:t0 + 512],
                 start=True, stop=True, inc=("tm", 1))
            tmv = k.cnt("tm")
            k.copy("V", agf[:, :], pA, waits=[("tm", tmv)])
            k.copy("V", auf[:, :], pB, inc=("vm", 1))
            vv0 = k.cnt("vm")
            for oc in range(IC):
                k.mm(pA, ewv[:, RG + oc * 128:RG + (oc + 1) * 128], agf[:, :],
                     start=True, stop=True, waits=[("vm", vv0)], inc=("tm", 1))
                k.mm(pB, ewv[:, 2 * RG + I0 + oc * 128:2 * RG + I0 + (oc + 1) * 128],
                     auf[:, :], start=True, stop=True, inc=("tm", 1))
                tmv = k.cnt("tm")
                k.act(gslt[:, :], pA, AF.Silu, waits=[("tm", tmv)], inc=("sm", 1))
                k.tt("V", prodt[:, :], gslt[:, :], pB, ALU.mult,
                     waits=[("sm", k.cnt("sm"))], inc=("vm", 1))
                k.mm(pR, sl3(rdt, 128)[:, oc, :], prodt[:, :],
                     start=(oc == 0), stop=(oc == IC - 1),
                     waits=[("vm", k.cnt("vm")), tokr],
                     inc=("tm", 1) if oc == IC - 1 else None)
            k.copy("V", rdsb[:, :], pR, waits=[("tm", k.cnt("tm"))], inc=("vm", 1))
            k.mm(pD, onerow_bf[0:1, :], wrow[0:1, t0:t0 + 512], start=True, stop=True,
                 waits=[rtok, ("vm", k.cnt("vm"))], inc=("tm", 1))
            k.tt("V", rdsb[:, :], rdsb[:, :], pD, ALU.mult,
                 waits=[("tm", k.cnt("tm"))], inc=("vm", 1))
            k.mm(pA, ewv[:, 2 * (RG + I0):2 * (RG + I0) + RG], rdsb[:, :],
                 start=True, stop=True, waits=[("vm", k.cnt("vm"))], inc=("tm", 1))
            k.act(h2t[:, :], pA, AF.Identity,
                  waits=[("tm", k.cnt("tm"))] + ([hprev[0]] if hprev[0] else []),
                  inc=("sm", 1))
            hprev[0] = k.dma("Y", ph2[e][:, t0:t0 + 512], h2t[:, :],
                             waits=[("sm", k.cnt("sm"))])

    k.hard()
    scd = k.sb("scd", 128, DC * 128, BF16)
    k.dma("G", sl3(scd, 128), k.dram["s_cdT"].rearrange("p (c o) -> p c o", o=128))
    Hbuf = xh[:, 0:E * 512]
    wd2 = xh[:, E * 512:E * 512 + 2 * E * 128]
    xfb = k.sb("xfb", 128, 2 * 512, F32)
    ob = k.sb("ob", 128, 2 * 512, F32)
    phr = h2d.rearrange("(r p) t -> p r t", p=128)
    pwr = cdb.rearrange("(r p) d -> p r d", p=128)
    for tg in range(4):
        t0 = tg * 512
        k.hard()
        htok1 = k.dma("G", Hbuf.rearrange("p (r t) -> p r t", t=512)[:, 0:16, :],
                      phr[:, 0:16, t0:t0 + 512])
        htok2 = k.dma("G", Hbuf.rearrange("p (r t) -> p r t", t=512)[:, 16:32, :],
                      phr[:, 16:32, t0:t0 + 512])
        wtok = [None, None]
        evv = [None, None]
        otok = [None, None]
        for ocd in range(DC):
            pp = ocd % 2
            wv = wd2.rearrange("p (b r o) -> p b r o", b=2, o=128)[:, pp]
            wtok[pp] = k.dma("G", wv, pwr[:, :, ocd * 128:(ocd + 1) * 128],
                             waits=[("tm", k.cnt("tm"))])
            xtok = k.dma("Y", xfb[:, pp * 512:pp * 512 + 512],
                         pct(xin)[:, ocd, t0:t0 + 512])
            psl = ps_sw[:, (2 + pp) * 512:(2 + pp) * 512 + 512]
            for ec in range(E):
                k.mm(psl, wv[:, ec, :],
                     Hbuf.rearrange("p (r t) -> p r t", t=512)[:, ec, :],
                     start=(ec == 0), stop=False,
                     waits=(([htok1, htok2, wtok[pp]] +
                             ([evv[pp]] if evv[pp] else [])) if ec == 0 else None))
            k.mm(psl, sl3(scd, 128)[:, ocd, :], hshf[:, t0:t0 + 512],
                 start=False, stop=True, inc=("tm", 1))
            k.tt("V", ob[:, pp * 512:pp * 512 + 512], psl, xfb[:, pp * 512:pp * 512 + 512],
                 ALU.add, waits=[("tm", k.cnt("tm")), xtok] +
                               ([otok[pp]] if otok[pp] else []),
                 inc=("vm", 1))
            evv[pp] = ("vm", k.cnt("vm"))
            otok[pp] = k.dma("Y", pct(xo)[:, ocd, t0:t0 + 512],
                             ob[:, pp * 512:pp * 512 + 512],
                             waits=[("vm", k.cnt("vm"))])
    k.hard()
    return k



def build_gqa_a(dump=()):
    k = K("gqa_a")
    xin = k.din("x", [D, TOWN])
    k.din("wq", [D, D])
    k.din("wk", [D, 512])
    k.din("wv", [D, 512])
    k.din("c_ones", [128])
    qd = k.dout("qd", [D, TOWN], BF16)
    kd = k.dout("kd", [512, TOWN], BF16)
    vd = k.dout("vd", [TOWN, 512], BF16)
    ln = LNKit(k)
    xh = k.sb("xh", 128, DC * TOWN, BF16)
    ps_sw = k.ps("ps_sw", 128, 4 * 512, F32)
    ln_prologue(k, ln, xin, xh, TOWN)
    wqs = k.sb("wqs", 128, DC * 512, BF16)
    ob = k.sb("ob", 128, 2 * 512, BF16)
    SC = 1.0 / np.sqrt(128.0)
    k.hard()
    otok = [None, None]

    def fm_proj(wname, out_d, oc0, scale):
        wtok = k.dma("G", sl3(wqs, 512),
                     pct(k.dram[wname])[:, :, oc0 * 128:oc0 * 128 + 512])
        for ocp in range(4):
            for tg in range(4):
                t0 = tg * 512
                pp = (ocp * 4 + tg) % 2
                psl = ps_sw[:, pp * 512:pp * 512 + 512]
                for kc in range(DC):
                    k.mm(psl, sl3(wqs, 512)[:, kc, ocp * 128:(ocp + 1) * 128],
                         sl3(xh, TOWN)[:, kc, t0:t0 + 512],
                         start=(kc == 0), stop=(kc == DC - 1),
                         waits=([wtok, ("sm", k.cnt("sm"))] if kc == 0 else None),
                         inc=("tm", 1) if kc == DC - 1 else None)
                k.act(ob[:, pp * 512:pp * 512 + 512], psl, AF.Identity, scale=scale,
                      waits=[("tm", k.cnt("tm"))] + ([otok[pp]] if otok[pp] else []),
                      inc=("sm", 1))
                otok[pp] = k.dma("Y", pct(out_d)[:, oc0 + ocp, t0:t0 + 512],
                                 ob[:, pp * 512:pp * 512 + 512],
                                 waits=[("sm", k.cnt("sm"))])

    for oc0 in range(0, DC, 4):
        k.hard()
        fm_proj("wq", qd, oc0, SC)
    k.hard()
    fm_proj("wk", kd, 0, 1.0)
    k.hard()
    wtok = k.dma("G", sl3(wqs, 512)[:, :, 0:512], pct(k.dram["wv"]))
    pvd = vd.rearrange("(tt p) o -> tt p o", p=128)
    for tt in range(DC):
        pp = tt % 2
        psl = ps_sw[:, pp * 512:pp * 512 + 512]
        for kc in range(DC):
            k.mm(psl, sl3(xh, TOWN)[:, kc, tt * 128:(tt + 1) * 128],
                 sl3(wqs, 512)[:, kc, 0:512],
                 start=(kc == 0), stop=(kc == DC - 1),
                 waits=([wtok, ("sm", k.cnt("sm"))] if kc == 0 else None),
                 inc=("tm", 1) if kc == DC - 1 else None)
        k.act(ob[:, pp * 512:pp * 512 + 512], psl, AF.Identity,
              waits=[("tm", k.cnt("tm"))] + ([otok[pp]] if otok[pp] else []),
              inc=("sm", 1))
        otok[pp] = k.dma("Y", pvd[tt], ob[:, pp * 512:pp * 512 + 512],
                         waits=[("sm", k.cnt("sm"))])
    k.hard()
    return k


def build_gqa_b(dump=()):
    k = K("gqa_b")
    HEADS, KVG = 16, 4
    xin = k.din("x", [D, TOWN])
    k.din("qd", [D, TOWN], BF16)
    k.din("kf", [512, T], BF16)
    k.din("vf", [T, 512], BF16)
    k.din("wo", [D, D])
    k.din("bo", [D])
    k.din("masks", [7 * 128, 512])
    k.din("kvalid", [128, 16])
    k.din("c_ones", [128])
    xo = k.dout("xo", [D, TOWN])

    ones_bf = k.sb("ones_bf", 128, 1, BF16)
    onerow_f = k.sb("onerow_f", 1, 128, F32)
    t1_ = k.dma("G", ones_bf[:, 0:1], k.dram["c_ones"].rearrange("(p o) -> p o", p=128, o=1))
    t2_ = k.dma("G", onerow_f[0:1, :], k.dram["c_ones"].rearrange("(o p) -> o p", o=1, p=128))
    k.wait("T", *t1_)
    k.wait("T", *t2_)
    ps_sw = k.ps("ps_sw", 128, 4 * 512, F32)
    ps_o = k.ps("ps_o", 128, 512, F32)
    ps_d = k.ps("ps_d", 1, 512, F32)
    ps_b2 = k.ps("ps_b2", 128, 512, F32)

    kfs = k.sb("kfs", 128, 4 * T, BF16)
    mks = k.sb("mks", 128, 7 * 512, BF16)
    kvc = k.sb("kvc", 128, 16, F32)
    qs = k.sb("qs", 128, 2 * 512, BF16)
    vts = k.sb("vts", 128, 32 * 128, BF16)
    pts = k.sb("pts", 128, 11 * 512, BF16)
    deni = k.sb("deni", 1, 512, F32)
    dbc = k.sb("dbc", 128, 512, F32)
    att = k.sb("att", 128, DC * TOWN, BF16)
    wqb = k.sb("wqb", 128, DC * 512, BF16)
    bocol = k.sb("bocol", 128, DC, F32)
    xfb = k.sb("xfb", 128, 2 * 512, F32)
    ob = k.sb("ob", 128, 2 * 512, F32)
    ktok = k.dma("G", sl3(kfs, T), k.dram["kf"].rearrange("(g p) t -> p g t", p=128))
    mtok = k.dma("G", sl3(mks, 512), k.dram["masks"].rearrange("(m p) t -> p m t", p=128))
    kvtok = k.dma("G", kvc[:, :], k.dram["kvalid"])
    bv = k.dram["bo"].rearrange("(c p o) -> c p o", p=128, o=1)
    for c in range(DC):
        k.dma("G", bocol[:, c:c + 1], bv[c])
    k.wait("T", *ktok)
    k.wait("T", *mtok)
    k.wait("V", *kvtok)

    QG = 2048
    pvf = k.dram["vf"].rearrange("(tt p) o -> p tt o", p=128)
    for h in range(HEADS):
        g = h // 4
        if h % 4 == 0:
            vtok1 = k.dma("G", vts[:, 0:16 * 128].rearrange("p (tt o) -> p tt o", o=128),
                          pvf[:, 0:16, g * 128:(g + 1) * 128],
                          waits=[("tm", k.cnt("tm"))])
            vtok2 = k.dma("G", vts[:, 16 * 128:32 * 128].rearrange("p (tt o) -> p tt o", o=128),
                          pvf[:, 16:32, g * 128:(g + 1) * 128])
        for qg in range(4):
            q0 = qg * 512
            q0g = QG + q0
            qpp = (h * 4 + qg) % 2
            qtok = k.dma("G", qs[:, qpp * 512:qpp * 512 + 512],
                         pct(k.dram["qd"])[:, h, q0:q0 + 512],
                         waits=[("tm", k.cnt("tm"))])
            qsl = qs[:, qpp * 512:qpp * 512 + 512]
            kts = [(kt, q0g - 128 * kt) for kt in range(T // 128)
                   if -384 <= q0g - 128 * kt <= 896]
            nkt = len(kts)
            lastw = []
            for i, (kt, a) in enumerate(kts):
                psl = ps_sw[:, (i % 2) * 512:(i % 2) * 512 + 512]
                k.mm(psl, sl3(kfs, T)[:, g, kt * 128:(kt + 1) * 128], qsl,
                     start=True, stop=True,
                     waits=[qtok, ("sm", k.cnt("sm")), ("vm", k.cnt("vm"))],
                     inc=("tm", 1))
                pt = pts[:, i * 512:(i + 1) * 512]
                k.act(pt, psl, AF.Exp, waits=[("tm", k.cnt("tm"))], inc=("sm", 1))
                last = ("sm", k.cnt("sm"))
                if a <= 0 or a > 512:
                    mi = (a + 384) // 128 if a <= 0 else 4 + (a - 640) // 128
                    k.tt("V", pt, pt, sl3(mks, 512)[:, mi, :], ALU.mult,
                         waits=[last], inc=("vm", 1))
                    last = ("vm", k.cnt("vm"))
                if kt < 16:
                    k.ts("V", pt, pt, kvc[:, kt:kt + 1], None, ALU.mult,
                         waits=[last], inc=("vm", 1))
                    last = ("vm", k.cnt("vm"))
                lastw.append(last)
                k.mm(ps_d[0:1, 0:512], ones_bf[:, 0:1], pt,
                     start=(i == 0), stop=(i == nkt - 1),
                     waits=[last] + ([("vm", k.cnt("vm"))] if i == 0 else []),
                     inc=("tm", 1) if i == nkt - 1 else None)
            k.recip(deni[0:1, 0:512], ps_d[0:1, 0:512], waits=[("tm", k.cnt("tm"))],
                    inc=("vm", 1))
            k.mm(ps_b2[:, 0:512], onerow_f[0:1, :], deni[0:1, 0:512],
                 start=True, stop=True, waits=[("vm", k.cnt("vm"))], inc=("tm", 1))
            k.copy("V", dbc[:, 0:512], ps_b2[:, 0:512], waits=[("tm", k.cnt("tm"))],
                   inc=("vm", 1))
            for i, (kt, a) in enumerate(kts):
                k.mm(ps_o[:, 0:512],
                     vts[:, 0:32 * 128].rearrange("p (tt o) -> p tt o", o=128)[:, kt, :],
                     pts[:, i * 512:(i + 1) * 512],
                     start=(i == 0), stop=(i == nkt - 1),
                     waits=([vtok1, vtok2, ("vm", k.cnt("vm"))] if i == 0 else None),
                     inc=("tm", 1) if i == nkt - 1 else None)
            k.tt("V", sl3(att, TOWN)[:, h, q0:q0 + 512], ps_o[:, 0:512], dbc[:, 0:512],
                 ALU.mult, waits=[("tm", k.cnt("tm"))], inc=("vm", 1))

    otok = [None, None]
    for oc0 in range(0, DC, 4):
        k.hard()
        wtok = k.dma("G", sl3(wqb, 512), pct(k.dram["wo"])[:, :, oc0 * 128:oc0 * 128 + 512])
        for ocp in range(4):
            oc = oc0 + ocp
            for tg in range(4):
                t0 = tg * 512
                pp = (ocp * 4 + tg) % 2
                psl = ps_sw[:, pp * 512:pp * 512 + 512]
                xtok = k.dma("Y", xfb[:, pp * 512:pp * 512 + 512],
                             pct(xin)[:, oc, t0:t0 + 512])
                for kc in range(DC):
                    k.mm(psl, sl3(wqb, 512)[:, kc, ocp * 128:(ocp + 1) * 128],
                         sl3(att, TOWN)[:, kc, t0:t0 + 512],
                         start=(kc == 0), stop=(kc == DC - 1),
                         waits=([wtok, ("vm", k.cnt("vm"))] if kc == 0 else None),
                         inc=("tm", 1) if kc == DC - 1 else None)
                k.act(ob[:, pp * 512:pp * 512 + 512], psl, AF.Identity,
                      bias=bocol[:, oc:oc + 1],
                      waits=[("tm", k.cnt("tm")), ("vm", k.cnt("vm"))] +
                            ([otok[pp]] if otok[pp] else []),
                      inc=("sm", 1))
                k.tt("V", ob[:, pp * 512:pp * 512 + 512], ob[:, pp * 512:pp * 512 + 512],
                     xfb[:, pp * 512:pp * 512 + 512], ALU.add,
                     waits=[("sm", k.cnt("sm")), xtok],
                     inc=("vm", 1))
                otok[pp] = k.dma("Y", pct(xo)[:, oc, t0:t0 + 512],
                                 ob[:, pp * 512:pp * 512 + 512],
                                 waits=[("vm", k.cnt("vm"))])
    k.hard()
    return k



def build_light_a(dump=()):
    k = K("light_a")
    H32, DH, BS = 32, 64, 256
    NB = TOWN // BS
    xin = k.din("x", [D, TOWN])
    k.din("qT", [D, D])
    k.din("kT", [D, D])
    k.din("vT", [D, D])
    k.din("gT", [D, D])
    k.din("ddT", [H32 * BS, BS])
    k.din("qdb", [H32 * DH, BS])
    k.din("kdb", [H32 * DH, BS])
    k.din("bdp", [H32])
    k.din("c_ones", [128])
    k.din("c_eyeb", [128, 128])
    qd_ = k.dout("qd", [D, TOWN], BF16)
    gd_ = k.dout("gd", [D, TOWN], BF16)
    attnr = k.dout("attnr", [D, TOWN], BF16)
    statec = k.dout("statec", [128, 16 * DH])
    kd_ = k.dtmp("kd_", [D, TOWN], BF16)
    vd_ = k.dtmp("vd_", [TOWN, D], BF16)

    ln = LNKit(k)
    xh = k.sb("xh", 128, DC * TOWN, BF16)
    ps_sw = k.ps("ps_sw", 128, 4 * 512, F32)
    ps_tr = k.ps("ps_tr", 128, 128, BF16)
    eyeb = k.sb("eyeb", 128, 128, BF16)
    k.dma("G", eyeb[:, :], k.dram["c_eyeb"])
    ln_prologue(k, ln, xin, xh, TOWN)
    wqs = k.sb("wqs", 128, DC * 512, BF16)
    ob = k.sb("ob", 128, 2 * 512, BF16)
    otok = [None, None]

    def fm_proj_full(wname, out_d, func):
        for oc0 in range(0, DC, 4):
            k.hard()
            wtok = k.dma("G", sl3(wqs, 512),
                         pct(k.dram[wname])[:, :, oc0 * 128:oc0 * 128 + 512])
            for ocp in range(4):
                for tg in range(4):
                    t0 = tg * 512
                    pp = (ocp * 4 + tg) % 2
                    psl = ps_sw[:, pp * 512:pp * 512 + 512]
                    for kc in range(DC):
                        k.mm(psl, sl3(wqs, 512)[:, kc, ocp * 128:(ocp + 1) * 128],
                             sl3(xh, TOWN)[:, kc, t0:t0 + 512],
                             start=(kc == 0), stop=(kc == DC - 1),
                             waits=([wtok, ("sm", k.cnt("sm"))] if kc == 0 else None),
                             inc=("tm", 1) if kc == DC - 1 else None)
                    k.act(ob[:, pp * 512:pp * 512 + 512], psl, func,
                          waits=[("tm", k.cnt("tm"))] + ([otok[pp]] if otok[pp] else []),
                          inc=("sm", 1))
                    otok[pp] = k.dma("Y", pct(out_d)[:, oc0 + ocp, t0:t0 + 512],
                                     ob[:, pp * 512:pp * 512 + 512],
                                     waits=[("sm", k.cnt("sm"))])

    fm_proj_full("qT", qd_, AF.Silu)
    fm_proj_full("kT", kd_, AF.Silu)
    fm_proj_full("gT", gd_, AF.Sigmoid)
    k.hard()
    pvd = vd_.rearrange("(tt p) o -> tt p o", p=128)
    for og in range(4):
        k.hard()
        wtok = k.dma("G", sl3(wqs, 512), pct(k.dram["vT"])[:, :, og * 512:(og + 1) * 512])
        for tt in range(DC):
            pp = tt % 2
            psl = ps_sw[:, pp * 512:pp * 512 + 512]
            for kc in range(DC):
                k.mm(psl, sl3(xh, TOWN)[:, kc, tt * 128:(tt + 1) * 128],
                     sl3(wqs, 512)[:, kc, 0:512],
                     start=(kc == 0), stop=(kc == DC - 1),
                     waits=([wtok, ("sm", k.cnt("sm"))] if kc == 0 else None),
                     inc=("tm", 1) if kc == DC - 1 else None)
            k.act(ob[:, pp * 512:pp * 512 + 512], psl, AF.Silu,
                  waits=[("tm", k.cnt("tm"))] + ([otok[pp]] if otok[pp] else []),
                  inc=("sm", 1))
            otok[pp] = k.dma("Y", pvd[tt][:, og * 512:(og + 1) * 512],
                             ob[:, pp * 512:pp * 512 + 512],
                             waits=[("sm", k.cnt("sm"))])

    k.hard()
    qdbs = k.sb("qdbs", 128, DC * BS, BF16)
    kdbs = k.sb("kdbs", 128, DC * BS, BF16)
    k.dma("G", sl3(qdbs, BS), k.dram["qdb"].rearrange("(c p) t -> p c t", p=128))
    kdt = k.dma("G", sl3(kdbs, BS), k.dram["kdb"].rearrange("(c p) t -> p c t", p=128))
    stf = k.sb("stf", 128, DC * DH, F32)
    stb = k.sb("stb", 128, DC * DH, BF16)
    k.memset("V", stf[:, :], 0.0)
    k.memset("V", stb[:, :], 0.0, inc=("vm", 1))
    qs = k.sb("qs", 128, TOWN, BF16)
    ks = k.sb("ks", 128, TOWN, BF16)
    vs = k.sb("vs", 128, DC * 128, BF16)
    ddb = k.sb("ddb", 128, 2 * 2 * BS, BF16)
    awb = k.sb("awb", 128, 2 * BS, BF16)
    qqdb = k.sb("qqdb", 128, BS, BF16)
    kkdb = k.sb("kkdb", 128, BS, BF16)
    kkdT = k.sb("kkdT", 128, 2 * DH, BF16)
    arow = k.sb("arow", 128, BS, BF16)
    aprev = [None]
    pdd = k.dram["ddT"].rearrange("(h j p) i -> h j p i", j=2, p=128)
    pqd = qd_.rearrange("(c p) t -> c p t", p=128)
    pkd = kd_.rearrange("(c p) t -> c p t", p=128)
    pvd2 = vd_.rearrange("(tt p) o -> p tt o", p=128)
    pat = attnr.rearrange("(c p) t -> c p t", p=128)
    for h in range(H32):
        hp = h // 2
        hb64 = (h % 2) * 64
        if h % 2 == 0:
            k.hard()
            t1 = k.dma("G", qs[:, :], pqd[hp])
            t2 = k.dma("G", ks[:, :], pkd[hp])
            t3 = k.dma("G", vs[:, :].rearrange("p (tt o) -> p tt o", o=128),
                       pvd2[:, :, hp * 128:(hp + 1) * 128])
            k.wait("T", *t1)
            k.wait("T", *t2)
            k.wait("T", *t3)
            k.wait("V", *t1)
            k.wait("V", *t2)
        ddtok0 = k.dma("G", ddb[:, (h % 2) * 2 * BS:(h % 2) * 2 * BS + BS],
                       pdd[h, 0], waits=[("vm", k.cnt("vm"))])
        ddtok = k.dma("G", ddb[:, (h % 2) * 2 * BS + BS:(h % 2) * 2 * BS + 2 * BS],
                      pdd[h, 1])
        k.wait("V", *ddtok0)
        k.wait("V", *ddtok)
        bd_h = None
        for blk in range(NB):
            j0 = blk * BS
            pI = ps_sw[:, 2 * 512:2 * 512 + BS]
            for jt in range(2):
                psl = ps_sw[:, jt * 512:jt * 512 + BS]
                k.mm(psl, ks[hb64:hb64 + 64, j0 + jt * 128:j0 + (jt + 1) * 128],
                     qs[hb64:hb64 + 64, j0:j0 + BS], start=True, stop=True,
                     waits=[("vm", k.cnt("vm")), ("sm", k.cnt("sm"))], inc=("tm", 1))
                k.tt("V", awb[:, jt * BS:(jt + 1) * BS], psl,
                     ddb[:, (h % 2) * 2 * BS + jt * BS:(h % 2) * 2 * BS + (jt + 1) * BS],
                     ALU.mult, waits=[("tm", k.cnt("tm"))], inc=("vm", 1))
            k.tt("V", qqdb[hb64:hb64 + 64, :], qs[hb64:hb64 + 64, j0:j0 + BS],
                 sl3(qdbs, BS)[hb64:hb64 + 64, hp, :], ALU.mult, inc=("vm", 1))
            vv = k.cnt("vm")
            for jt in range(2):
                k.mm(pI[0:64, :], vs[:, :].rearrange("p (tt o) -> p tt o", o=128)
                     [:, (j0 + jt * 128) // 128, hb64:hb64 + 64],
                     awb[:, jt * BS:(jt + 1) * BS],
                     start=(jt == 0), stop=False,
                     waits=[("vm", vv)] if jt == 0 else None)
            k.mm(pI[0:64, :], stb[hb64:hb64 + 64, hp * DH:hp * DH + DH],
                 qqdb[hb64:hb64 + 64, :], start=False, stop=True, inc=("tm", 1))
            k.act(arow[0:64, :], pI[0:64, :], AF.Identity,
                  waits=[("tm", k.cnt("tm")), ("vm", k.cnt("vm"))] +
                        ([aprev[0]] if aprev[0] else []),
                  inc=("sm", 1))
            aprev[0] = k.dma("Y", pat[hp][hb64:hb64 + 64, j0:j0 + BS], arow[0:64, :],
                             waits=[("sm", k.cnt("sm"))])
            k.tt("V", kkdb[hb64:hb64 + 64, :], ks[hb64:hb64 + 64, j0:j0 + BS],
                 sl3(kdbs, BS)[hb64:hb64 + 64, hp, :], ALU.mult, inc=("vm", 1))
            for jt in range(2):
                k.transp(ps_tr[:, 0:64], kkdb[hb64:hb64 + 64, jt * 128:(jt + 1) * 128],
                         eyeb[hb64:hb64 + 64, hb64:hb64 + 64],
                         waits=[("vm", k.cnt("vm"))], inc=("tm", 1))
                k.copy("V", kkdT[:, jt * DH:(jt + 1) * DH], ps_tr[:, 0:64],
                       waits=[("tm", k.cnt("tm"))], inc=("vm", 1))
            pK = ps_sw[:, 3 * 512:3 * 512 + DH]
            for jt in range(2):
                k.mm(pK[0:64, :], kkdT[:, jt * DH:(jt + 1) * DH],
                     vs[:, :].rearrange("p (tt o) -> p tt o", o=128)
                     [:, (j0 + jt * 128) // 128, hb64:hb64 + 64],
                     start=(jt == 0), stop=(jt == 1),
                     waits=[("vm", k.cnt("vm"))] if jt == 0 else None,
                     inc=("tm", 1) if jt == 1 else None)
            k.stt("V", stf[hb64:hb64 + 64, hp * DH:hp * DH + DH],
                  stf[hb64:hb64 + 64, hp * DH:hp * DH + DH],
                  float(np.exp(-SLOPES[h] * BS)), pK[0:64, :],
                  ALU.mult, ALU.add, waits=[("tm", k.cnt("tm"))], inc=("vm", 1))
            k.copy("V", stb[hb64:hb64 + 64, hp * DH:hp * DH + DH],
                   stf[hb64:hb64 + 64, hp * DH:hp * DH + DH], inc=("vm", 1))
    k.wait("Y", "vm", k.cnt("vm"))
    k.dma("Y", statec, stf[:, :])
    k.hard()
    return k


SLOPES = [float((1.0 / 2.0 ** (8.0 / 32)) ** (hh + 1) * (1.0 - 0.0 / (32 - 1 + 1e-5) + 1e-5))
          for hh in range(32)]


def build_light_b(dump=()):
    k = K("light_b")
    H32, DH, BS = 32, 64, 256
    NB = TOWN // BS
    xin = k.din("x", [D, TOWN])
    k.din("attnr", [D, TOWN], BF16)
    k.din("qd", [D, TOWN], BF16)
    k.din("gd", [D, TOWN], BF16)
    k.din("statec", [128, 16 * DH])
    k.din("qdb", [H32 * DH, BS])
    k.din("outT", [D, D])
    k.din("normw", [D])
    k.din("c_ones", [128])
    xo = k.dout("xo", [D, TOWN])

    ln = LNKit(k)
    ps_sw = k.ps("ps_sw", 128, 4 * 512, F32)
    att = k.sb("att", 128, DC * TOWN, BF16)
    qdbs = k.sb("qdbs", 128, DC * BS, BF16)
    k.dma("G", sl3(qdbs, BS), k.dram["qdb"].rearrange("(c p) t -> p c t", p=128))
    stf = k.sb("stf", 128, DC * DH, F32)
    sef = k.sb("sef", 128, DH, F32)
    sefb = k.sb("sefb", 128, DH, BF16)
    stok = k.dma("G", stf[:, :], k.dram["statec"])
    k.wait("V", *stok)
    qs = k.sb("qs", 128, TOWN, BF16)
    qqdb = k.sb("qqdb", 128, BS, BF16)
    nwcol = k.sb("nwcol", 128, DC, F32)
    bv = k.dram["normw"].rearrange("(c p o) -> c p o", p=128, o=1)
    for c in range(DC):
        k.dma("G", nwcol[:, c:c + 1], bv[c])
    pqd = qd_view = k.dram["qd"].rearrange("(c p) t -> c p t", p=128)
    pat = k.dram["attnr"].rearrange("(c p) t -> c p t", p=128)
    for c in range(DC):
        tok = k.dma("G", sl3(att, TOWN)[:, c, :], pat[c])
    k.fence_tag("V", "gd")
    for h in range(H32):
        hp = h // 2
        hb64 = (h % 2) * 64
        if h % 2 == 0:
            k.hard()
            t1 = k.dma("G", qs[:, :], pqd[hp])
            k.wait("V", *t1)
        for blk in range(NB):
            j0 = blk * BS
            scale = float(np.exp(-SLOPES[h] * BS * blk))
            k.ts("V", sef[hb64:hb64 + 64, :], stf[hb64:hb64 + 64, hp * DH:hp * DH + DH],
                 scale, None, ALU.mult)
            k.copy("V", sefb[hb64:hb64 + 64, :], sef[hb64:hb64 + 64, :], inc=("vm", 1))
            k.tt("V", qqdb[hb64:hb64 + 64, :], qs[hb64:hb64 + 64, j0:j0 + BS],
                 sl3(qdbs, BS)[hb64:hb64 + 64, hp, :], ALU.mult, inc=("vm", 1))
            psl = ps_sw[:, (blk % 2) * 512:(blk % 2) * 512 + BS]
            k.mm(psl[0:64, :], sefb[hb64:hb64 + 64, :], qqdb[hb64:hb64 + 64, :],
                 start=True, stop=True, waits=[("vm", k.cnt("vm"))], inc=("tm", 1))
            k.tt("V", sl3(att, TOWN)[hb64:hb64 + 64, hp, j0:j0 + BS],
                 sl3(att, TOWN)[hb64:hb64 + 64, hp, j0:j0 + BS], psl[0:64, :],
                 ALU.add, waits=[("tm", k.cnt("tm"))], inc=("vm", 1))
    k.hard()
    for g0 in range(0, TOWN, GW):
        gw = GW
        k.fence("S", "vm")
        for c in range(DC):
            k.act(cview(ln.sqf, c, GW)[:, 0:gw], sl3(att, TOWN)[:, c, g0:g0 + gw],
                  AF.Square, inc=("sm", 1) if c == DC - 1 else None)
        vnow = k.cnt("vm")
        for c in range(DC):
            k.mm(ln.ps_s[0:1, 0:gw], ln.ones_bf[:, 0:1], cview(ln.sqf, c, GW)[:, 0:gw],
                 start=(c == 0), stop=(c == DC - 1),
                 waits=[("sm", k.cnt("sm")), ("vm", vnow)] if c == 0 else None,
                 inc=("tm", 1) if c == DC - 1 else None)
        k.ts("V", ln.tmp[0:1, 0:gw], ln.ps_s[0:1, 0:gw], 1.0 / D, float(EPS),
             ALU.mult, ALU.add, waits=[("tm", k.cnt("tm"))], inc=("vm", 1))
        k.act(ln.tmp2[0:1, 0:gw], ln.tmp[0:1, 0:gw], AF.Ln,
              waits=[("vm", k.cnt("vm"))])
        k.act(ln.rstd[0:1, 0:gw], ln.tmp2[0:1, 0:gw], AF.Exp, scale=-0.5, inc=("sm", 1))
        k.mm(ln.ps_b[:, 0:gw], ln.onerow_f[0:1, :], ln.rstd[0:1, 0:gw],
             start=True, stop=True, waits=[("sm", k.cnt("sm"))], inc=("tm", 1))
        k.copy("V", ln.rb[:, 0:gw], ln.ps_b[:, 0:gw], waits=[("tm", k.cnt("tm"))])
        for c in range(DC):
            k.tt("V", ln.lnt[:, 0:gw], sl3(att, TOWN)[:, c, g0:g0 + gw], ln.rb[:, 0:gw],
                 ALU.mult)
            k.ts("V", sl3(att, TOWN)[:, c, g0:g0 + gw], ln.lnt[:, 0:gw],
                 nwcol[:, c:c + 1], None, ALU.mult,
                 inc=("vm", 1) if c == DC - 1 else None)
    k.hard()
    gsl = k.sb("gsl", 128, 2 * 512, BF16)
    pgd = k.dram["gd"].rearrange("(c p) t -> c p t", p=128)
    gtoks = [None, None]
    for c in range(DC):
        for tg in range(4):
            pp = (c * 4 + tg) % 2
            tok = k.dma("G", gsl[:, pp * 512:(pp + 1) * 512],
                        pgd[c][:, tg * 512:(tg + 1) * 512],
                        waits=[gtoks[pp]] if gtoks[pp] else None)
            k.tt("V", sl3(att, TOWN)[:, c, tg * 512:(tg + 1) * 512],
                 sl3(att, TOWN)[:, c, tg * 512:(tg + 1) * 512],
                 gsl[:, pp * 512:(pp + 1) * 512], ALU.mult,
                 waits=[tok], inc=("vm", 1))
            gtoks[pp] = ("vm", k.cnt("vm"))
    k.hard()
    wqb = k.sb("wqb", 128, DC * 512, BF16)
    xfb = k.sb("xfb", 128, 2 * 512, F32)
    ob = k.sb("ob", 128, 2 * 512, F32)
    otok = [None, None]
    for oc0 in range(0, DC, 4):
        k.hard()
        wtok = k.dma("G", sl3(wqb, 512), pct(k.dram["outT"])[:, :, oc0 * 128:oc0 * 128 + 512])
        for ocp in range(4):
            oc = oc0 + ocp
            for tg in range(4):
                t0 = tg * 512
                pp = (ocp * 4 + tg) % 2
                psl = ps_sw[:, pp * 512:pp * 512 + 512]
                xtok = k.dma("Y", xfb[:, pp * 512:pp * 512 + 512],
                             pct(xin)[:, oc, t0:t0 + 512])
                for kc in range(DC):
                    k.mm(psl, sl3(wqb, 512)[:, kc, ocp * 128:(ocp + 1) * 128],
                         sl3(att, TOWN)[:, kc, t0:t0 + 512],
                         start=(kc == 0), stop=(kc == DC - 1),
                         waits=([wtok, ("vm", k.cnt("vm"))] if kc == 0 else None),
                         inc=("tm", 1) if kc == DC - 1 else None)
                k.tt("V", ob[:, pp * 512:pp * 512 + 512], psl,
                     xfb[:, pp * 512:pp * 512 + 512], ALU.add,
                     waits=[("tm", k.cnt("tm")), xtok] +
                           ([otok[pp]] if otok[pp] else []),
                     inc=("vm", 1))
                otok[pp] = k.dma("Y", pct(xo)[:, oc, t0:t0 + 512],
                                 ob[:, pp * 512:pp * 512 + 512],
                                 waits=[("vm", k.cnt("vm"))])
    k.hard()
    return k



def build_mla_a(dump=()):
    k = K("mla_a")
    xin = k.din("x", [D, TOWN])
    k.din("wdqT", [D, 512])
    k.din("wdkvT", [D, 512])
    k.din("krTe", [D, 32])
    k.din("krTo", [D, 32])
    k.din("cosf", [32, TOWN])
    k.din("sinf", [32, TOWN])
    k.din("c_ones", [128])
    cqd = k.dout("cqd", [512, TOWN], BF16)
    ckvd = k.dout("ckvd", [512, TOWN], BF16)
    krd = k.dout("krd", [64, TOWN], BF16)
    ln = LNKit(k)
    xh = k.sb("xh", 128, DC * TOWN, BF16)
    ps_sw = k.ps("ps_sw", 128, 4 * 512, F32)
    ln_prologue(k, ln, xin, xh, TOWN)
    wqs = k.sb("wqs", 128, DC * 512, BF16)
    ob = k.sb("ob", 128, 2 * 512, BF16)
    otok = [None, None]
    k.hard()
    for (wname, dst) in (("wdqT", cqd), ("wdkvT", ckvd)):
        k.hard()
        wtok = k.dma("G", sl3(wqs, 512), pct(k.dram[wname]))
        for ocp in range(4):
            for tg in range(4):
                t0 = tg * 512
                pp = (ocp * 4 + tg) % 2
                psl = ps_sw[:, pp * 512:pp * 512 + 512]
                for kc in range(DC):
                    k.mm(psl, sl3(wqs, 512)[:, kc, ocp * 128:(ocp + 1) * 128],
                         sl3(xh, TOWN)[:, kc, t0:t0 + 512],
                         start=(kc == 0), stop=(kc == DC - 1),
                         waits=([wtok, ("sm", k.cnt("sm"))] if kc == 0 else None),
                         inc=("tm", 1) if kc == DC - 1 else None)
                k.act(ob[:, pp * 512:pp * 512 + 512], psl, AF.Identity,
                      waits=[("tm", k.cnt("tm"))] + ([otok[pp]] if otok[pp] else []),
                      inc=("sm", 1))
                otok[pp] = k.dma("Y", pct(dst)[:, ocp, t0:t0 + 512],
                                 ob[:, pp * 512:pp * 512 + 512],
                                 waits=[("sm", k.cnt("sm"))])
    k.hard()
    kre = k.sb("kre", 32, TOWN, F32)
    kro = k.sb("kro", 32, TOWN, F32)
    cosb = k.sb("cosb", 32, TOWN, F32)
    sinb = k.sb("sinb", 32, TOWN, F32)
    rt1 = k.sb("rt1", 32, TOWN, F32)
    rt2 = k.sb("rt2", 32, TOWN, F32)
    rto = k.sb("rto", 32, 2 * TOWN, BF16)
    ct = k.dma("G", cosb[:, :], k.dram["cosf"])
    st = k.dma("G", sinb[:, :], k.dram["sinf"])
    k.wait("V", *ct)
    k.wait("V", *st)
    wr1 = k.dma("G", sl3(wqs, 512)[:, :, 0:32], pct(k.dram["krTe"]))
    wr2 = k.dma("G", sl3(wqs, 512)[:, :, 64:96], pct(k.dram["krTo"]))
    for tg in range(4):
        t0 = tg * 512
        pA = ps_sw[:, 0:512]
        pB = ps_sw[:, 512:1024]
        for kc in range(DC):
            k.mm(pA[0:32, :], sl3(wqs, 512)[:, kc, 0:32],
                 sl3(xh, TOWN)[:, kc, t0:t0 + 512],
                 start=(kc == 0), stop=(kc == DC - 1),
                 waits=([wr1, wr2, ("vm", k.cnt("vm"))] if kc == 0 else None))
            k.mm(pB[0:32, :], sl3(wqs, 512)[:, kc, 64:96],
                 sl3(xh, TOWN)[:, kc, t0:t0 + 512],
                 start=(kc == 0), stop=(kc == DC - 1),
                 inc=("tm", 1) if kc == DC - 1 else None)
        k.copy("V", kre[:, t0:t0 + 512], pA[0:32, :], waits=[("tm", k.cnt("tm"))])
        k.copy("V", kro[:, t0:t0 + 512], pB[0:32, :], inc=("vm", 1))
    k.tt("V", rt1[:, :], kre[:, :], cosb[:, :], ALU.mult)
    k.tt("V", rt2[:, :], kro[:, :], sinb[:, :], ALU.mult)
    k.tt("V", rt1[:, :], rt1[:, :], rt2[:, :], ALU.subtract)
    k.copy("V", rto[:, 0:TOWN], rt1[:, :])
    k.tt("V", rt1[:, :], kre[:, :], sinb[:, :], ALU.mult)
    k.tt("V", rt2[:, :], kro[:, :], cosb[:, :], ALU.mult)
    k.tt("V", rt1[:, :], rt1[:, :], rt2[:, :], ALU.add)
    k.copy("V", rto[:, TOWN:2 * TOWN], rt1[:, :], inc=("vm", 1))
    k.wait("Y", "vm", k.cnt("vm"))
    k.dma("Y", krd[0:32, :], rto[0:32, 0:TOWN])
    k.dma("Y", krd[32:64, :].rearrange("p t -> p t"), rto[0:32, TOWN:2 * TOWN])
    k.hard()
    return k


def build_mla_b(dump=()):
    k = K("mla_b")
    HH = 8
    k.din("cqf", [512, T], BF16)
    k.din("ckvf", [512, T], BF16)
    k.din("krf", [64, T], BF16)
    k.din("wuqT", [512, HH * 128])
    k.din("wqrTe", [512, HH * 32])
    k.din("wqrTo", [512, HH * 32])
    k.din("wukT", [512, HH * 128])
    k.din("wuvT", [512, HH * 128])
    k.din("cos4", [128, T])
    k.din("sin4", [128, T])
    k.din("cmasks", [4 * 128, 512])
    k.din("c_ones", [128])
    attd = k.dout("attd", [HH * 128, T], BF16)
    qcd2 = k.dtmp("qcd2", [HH * 128, T], BF16)
    kcd2 = k.dtmp("kcd2", [HH * 128, T], BF16)
    vtd2 = k.dtmp("vtd2", [T, HH * 128], BF16)
    qrd2 = k.dtmp("qrd2", [HH * 64, T], BF16)

    ones_bf = k.sb("ones_bf", 128, 1, BF16)
    onerow_f = k.sb("onerow_f", 1, 128, F32)
    t1_ = k.dma("G", ones_bf[:, 0:1], k.dram["c_ones"].rearrange("(p o) -> p o", p=128, o=1))
    t2_ = k.dma("G", onerow_f[0:1, :], k.dram["c_ones"].rearrange("(o p) -> o p", o=1, p=128))
    k.wait("T", *t1_)
    k.wait("T", *t2_)
    ps_sw = k.ps("ps_sw", 128, 4 * 512, F32)
    ps_o = k.ps("ps_o", 128, 512, F32)
    ps_d = k.ps("ps_d", 1, 512, F32)
    ps_b2 = k.ps("ps_b2", 128, 512, F32)
    SC = float(1.0 / np.sqrt(192.0))
    CLIP_HI = float(np.exp(80.0 / np.sqrt(192.0)))
    CLIP_LO = float(np.exp(-80.0 / np.sqrt(192.0)))

    cfs = k.sb("cfs", 128, 4 * T, BF16)
    wql = k.sb("wql", 128, 4 * 1024, BF16)
    ob = k.sb("ob", 128, 2 * 512, BF16)
    re1 = k.sb("re1", 128, 2 * 512, F32)
    ro1 = k.sb("ro1", 128, 2 * 512, F32)
    rr1 = k.sb("rr1", 128, 512, F32)
    rr2 = k.sb("rr2", 128, 512, F32)
    cosb = k.sb("cosb", 128, T, BF16)
    sinb = k.sb("sinb", 128, T, BF16)
    ct = k.dma("G", cosb[:, :], k.dram["cos4"])
    st = k.dma("G", sinb[:, :], k.dram["sin4"])
    k.wait("V", *ct)
    k.wait("V", *st)
    otok = [None, None]

    ctok = k.dma("G", sl3(cfs, T), k.dram["cqf"].rearrange("(c p) t -> p c t", p=128))

    def up_fm(wname, dst, scale):
        wtok = k.dma("G", wql.rearrange("p (c o) -> p c o", o=1024),
                     k.dram[wname].rearrange("(c p) o -> p c o", p=128),
                     waits=[("tm", k.cnt("tm"))])
        for oc in range(HH):
            for tg in range(8):
                t0 = tg * 512
                pp = (oc * 8 + tg) % 2
                psl = ps_sw[:, pp * 512:pp * 512 + 512]
                for kc in range(4):
                    k.mm(psl, wql.rearrange("p (c o) -> p c o", o=1024)
                         [:, kc, oc * 128:(oc + 1) * 128],
                         sl3(cfs, T)[:, kc, t0:t0 + 512],
                         start=(kc == 0), stop=(kc == 3),
                         waits=([wtok, ctok, ("sm", k.cnt("sm"))] if kc == 0 else None),
                         inc=("tm", 1) if kc == 3 else None)
                k.act(ob[:, pp * 512:pp * 512 + 512], psl, AF.Identity, scale=scale,
                      waits=[("tm", k.cnt("tm"))] + ([otok[pp]] if otok[pp] else []),
                      inc=("sm", 1))
                otok[pp] = k.dma("Y", pct(dst)[:, oc, t0:t0 + 512],
                                 ob[:, pp * 512:pp * 512 + 512],
                                 waits=[("sm", k.cnt("sm"))])

    up_fm("wuqT", qcd2, SC)
    k.hard()
    wre = k.dma("G", wql.rearrange("p (c o) -> p c o", o=1024)[:, :, 0:256],
                k.dram["wqrTe"].rearrange("(c p) o -> p c o", p=128))
    wro = k.dma("G", wql.rearrange("p (c o) -> p c o", o=1024)[:, :, 256:512],
                k.dram["wqrTo"].rearrange("(c p) o -> p c o", p=128))
    pqr = qrd2.rearrange("(h s q) t -> h s q t", s=2, q=32)
    for oc in range(2):
        for tg in range(8):
            t0 = tg * 512
            pA = ps_sw[:, 0:512]
            pB = ps_sw[:, 512:1024]
            for kc in range(4):
                k.mm(pA, wql.rearrange("p (c o) -> p c o", o=1024)
                     [:, kc, oc * 128:oc * 128 + 128],
                     sl3(cfs, T)[:, kc, t0:t0 + 512],
                     start=(kc == 0), stop=(kc == 3),
                     waits=([wre, wro, ("vm", k.cnt("vm"))] if kc == 0 else None))
                k.mm(pB, wql.rearrange("p (c o) -> p c o", o=1024)
                     [:, kc, 256 + oc * 128:256 + oc * 128 + 128],
                     sl3(cfs, T)[:, kc, t0:t0 + 512],
                     start=(kc == 0), stop=(kc == 3),
                     inc=("tm", 1) if kc == 3 else None)
            tmv = k.cnt("tm")
            k.copy("V", re1[:, 0:512], pA, waits=[("tm", tmv)])
            k.copy("V", ro1[:, 0:512], pB, inc=("vm", 1))
            k.tt("V", rr1[:, :], re1[:, 0:512], cosb[:, t0:t0 + 512], ALU.mult,
                 waits=([otok[0]] if otok[0] else []))
            k.tt("V", rr2[:, :], ro1[:, 0:512], sinb[:, t0:t0 + 512], ALU.mult)
            k.tt("V", rr1[:, :], rr1[:, :], rr2[:, :], ALU.subtract)
            k.ts("V", ob[:, 0:512], rr1[:, :], SC, None, ALU.mult, inc=("vm", 1))
            vv1 = k.cnt("vm")
            k.tt("V", rr1[:, :], re1[:, 0:512], sinb[:, t0:t0 + 512], ALU.mult,
                 waits=([otok[1]] if otok[1] else []))
            k.tt("V", rr2[:, :], ro1[:, 0:512], cosb[:, t0:t0 + 512], ALU.mult)
            k.tt("V", rr1[:, :], rr1[:, :], rr2[:, :], ALU.add)
            k.ts("V", ob[:, 512:1024], rr1[:, :], SC, None, ALU.mult, inc=("vm", 1))
            vv2 = k.cnt("vm")
            te = k.dma("Y", pqr[oc * 4:(oc + 1) * 4, 0, :, t0:t0 + 512],
                       ob[:, 0:512], waits=[("vm", vv1)])
            to = k.dma("Y", pqr[oc * 4:(oc + 1) * 4, 1, :, t0:t0 + 512],
                       ob[:, 512:1024], waits=[("vm", vv2)])
            otok[0] = te
            otok[1] = to
    k.hard()
    ctok = k.dma("G", sl3(cfs, T), k.dram["ckvf"].rearrange("(c p) t -> p c t", p=128),
                 waits=[("tm", k.cnt("tm"))])
    up_fm("wukT", kcd2, 1.0)
    k.hard()
    wtok = k.dma("G", wql.rearrange("p (c o) -> p c o", o=1024),
                 k.dram["wuvT"].rearrange("(c p) o -> p c o", p=128),
                 waits=[("tm", k.cnt("tm"))])
    pvt = vtd2.rearrange("(tt p) o -> tt p o", p=128)
    for tt in range(T // 128):
        for og in range(2):
            pp = (tt * 2 + og) % 2
            psl = ps_sw[:, pp * 512:pp * 512 + 512]
            for kc in range(4):
                k.mm(psl, sl3(cfs, T)[:, kc, tt * 128:(tt + 1) * 128],
                     wql.rearrange("p (c o) -> p c o", o=1024)[:, kc, og * 512:(og + 1) * 512],
                     start=(kc == 0), stop=(kc == 3),
                     waits=([wtok, ctok, ("sm", k.cnt("sm"))] if kc == 0 else None),
                     inc=("tm", 1) if kc == 3 else None)
            k.act(ob[:, pp * 512:pp * 512 + 512], psl, AF.Identity,
                  waits=[("tm", k.cnt("tm"))] + ([otok[pp]] if otok[pp] else []),
                  inc=("sm", 1))
            otok[pp] = k.dma("Y", pvt[tt][:, og * 512:(og + 1) * 512],
                             ob[:, pp * 512:pp * 512 + 512],
                             waits=[("sm", k.cnt("sm"))])

    k.hard()
    kcs = k.sb("kcs", 128, T, BF16)
    vts = k.sb("vts", 128, 32 * 128, BF16)
    krs = k.sb("krs", 64, T, BF16)
    qcs = k.sb("qcs", 128, 2 * 512, BF16)
    qrs = k.sb("qrs", 64, 2 * 512, BF16)
    mks = k.sb("mks", 128, 4 * 512, BF16)
    pts = k.sb("pts", 128, 32 * 512, BF16)
    deni = k.sb("deni", 1, 512, F32)
    dbc = k.sb("dbc", 128, 512, F32)
    orow = k.sb("orow", 128, 2 * 512, BF16)
    krtok = k.dma("G", krs[0:64, :], k.dram["krf"])
    mtok = k.dma("G", sl3(mks, 512), k.dram["cmasks"].rearrange("(m p) t -> p m t", p=128))
    k.wait("T", *krtok)
    k.wait("T", *mtok)
    pkc = kcd2.rearrange("(c p) t -> c p t", p=128)
    pqc = qcd2.rearrange("(c p) t -> c p t", p=128)
    pqr2 = qrd2.rearrange("(h p) t -> h p t", p=64)
    pvt2 = vtd2.rearrange("(tt p) o -> p tt o", p=128)
    pat = attd.rearrange("(c p) t -> c p t", p=128)
    for h in range(HH):
        k.hard()
        t1 = k.dma("G", kcs[:, :], pkc[h])
        t2 = k.dma("G", vts[:, :].rearrange("p (tt o) -> p tt o", o=128),
                   pvt2[:, :, h * 128:(h + 1) * 128])
        k.wait("T", *t1)
        k.wait("T", *t2)
        for qg in range(8):
            q0 = qg * 512
            qpp = qg % 2
            qtok1 = k.dma("G", qcs[:, qpp * 512:qpp * 512 + 512], pqc[h][:, q0:q0 + 512],
                          waits=[("tm", k.cnt("tm"))])
            qtok2 = k.dma("G", qrs[0:64, qpp * 512:qpp * 512 + 512],
                          pqr2[h][:, q0:q0 + 512])
            nkt = qg * 4 + 4
            lastw = []
            for kt in range(nkt):
                a = q0 - kt * 128
                psl = ps_sw[:, (kt % 2) * 512:(kt % 2) * 512 + 512]
                k.mm(psl, kcs[:, kt * 128:(kt + 1) * 128],
                     qcs[:, qpp * 512:qpp * 512 + 512],
                     start=True, stop=False,
                     waits=[qtok1, qtok2, ("sm", k.cnt("sm")), ("vm", k.cnt("vm"))])
                k.mm(psl, krs[0:64, kt * 128:(kt + 1) * 128],
                     qrs[0:64, qpp * 512:qpp * 512 + 512],
                     start=False, stop=True, inc=("tm", 1))
                pt = pts[:, kt * 512:(kt + 1) * 512]
                k.act(pt, psl, AF.Exp, waits=[("tm", k.cnt("tm"))], inc=("sm", 1))
                k.ts("V", pt, pt, CLIP_HI, CLIP_LO, ALU.min, ALU.max,
                     waits=[("sm", k.cnt("sm"))], inc=("vm", 1))
                last = ("vm", k.cnt("vm"))
                if -384 <= a <= 0:
                    mi = (-a) // 128
                    k.tt("V", pt, pt, sl3(mks, 512)[:, mi, :], ALU.mult,
                         waits=[last], inc=("vm", 1))
                    last = ("vm", k.cnt("vm"))
                lastw.append(last)
                k.mm(ps_d[0:1, 0:512], ones_bf[:, 0:1], pt,
                     start=(kt == 0), stop=(kt == nkt - 1),
                     waits=[last] + ([("vm", k.cnt("vm"))] if kt == 0 else []),
                     inc=("tm", 1) if kt == nkt - 1 else None)
            k.recip(deni[0:1, 0:512], ps_d[0:1, 0:512], waits=[("tm", k.cnt("tm"))],
                    inc=("vm", 1))
            k.mm(ps_b2[:, 0:512], onerow_f[0:1, :], deni[0:1, 0:512],
                 start=True, stop=True, waits=[("vm", k.cnt("vm"))], inc=("tm", 1))
            k.copy("V", dbc[:, 0:512], ps_b2[:, 0:512], waits=[("tm", k.cnt("tm"))],
                   inc=("vm", 1))
            for kt in range(nkt):
                k.mm(ps_o[:, 0:512],
                     vts[:, :].rearrange("p (tt o) -> p tt o", o=128)[:, kt, :],
                     pts[:, kt * 512:(kt + 1) * 512],
                     start=(kt == 0), stop=(kt == nkt - 1),
                     waits=[("vm", k.cnt("vm"))] if kt == 0 else None,
                     inc=("tm", 1) if kt == nkt - 1 else None)
            k.tt("V", orow[:, qpp * 512:qpp * 512 + 512], ps_o[:, 0:512], dbc[:, 0:512],
                 ALU.mult, waits=[("tm", k.cnt("tm"))] +
                                 ([otok[qpp]] if otok[qpp] else []),
                 inc=("vm", 1))
            otok[qpp] = k.dma("Y", pat[h][:, q0:q0 + 512],
                              orow[:, qpp * 512:qpp * 512 + 512],
                              waits=[("vm", k.cnt("vm"))])
    k.hard()
    return k


def build_mla_c(dump=()):
    k = K("mla_c")
    xin = k.din("x", [D, TOWN])
    k.din("attf", [D, TOWN], BF16)
    k.din("woT", [D, D])
    xo = k.dout("xo", [D, TOWN])
    att = k.sb("att", 128, DC * TOWN, BF16)
    wqb = k.sb("wqb", 128, DC * 512, BF16)
    xfb = k.sb("xfb", 128, 2 * 512, F32)
    ob = k.sb("ob", 128, 2 * 512, F32)
    ps_sw = k.ps("ps_sw", 128, 4 * 512, F32)
    pa = k.dram["attf"].rearrange("(c p) t -> c p t", p=128)
    for c in range(DC):
        k.dma("G", sl3(att, TOWN)[:, c, :], pa[c])
    otok = [None, None]
    for oc0 in range(0, DC, 4):
        k.hard()
        wtok = k.dma("G", sl3(wqb, 512), pct(k.dram["woT"])[:, :, oc0 * 128:oc0 * 128 + 512])
        for ocp in range(4):
            oc = oc0 + ocp
            for tg in range(4):
                t0 = tg * 512
                pp = (ocp * 4 + tg) % 2
                psl = ps_sw[:, pp * 512:pp * 512 + 512]
                xtok = k.dma("Y", xfb[:, pp * 512:pp * 512 + 512],
                             pct(xin)[:, oc, t0:t0 + 512])
                for kc in range(DC):
                    k.mm(psl, sl3(wqb, 512)[:, kc, ocp * 128:(ocp + 1) * 128],
                         sl3(att, TOWN)[:, kc, t0:t0 + 512],
                         start=(kc == 0), stop=(kc == DC - 1),
                         waits=([wtok, ("vm", k.cnt("vm"))] if kc == 0 else None),
                         inc=("tm", 1) if kc == DC - 1 else None)
                k.tt("V", ob[:, pp * 512:pp * 512 + 512], psl,
                     xfb[:, pp * 512:pp * 512 + 512], ALU.add,
                     waits=[("tm", k.cnt("tm")), xtok] +
                           ([otok[pp]] if otok[pp] else []),
                     inc=("vm", 1))
                otok[pp] = k.dma("Y", pct(xo)[:, oc, t0:t0 + 512],
                                 ob[:, pp * 512:pp * 512 + 512],
                                 waits=[("vm", k.cnt("vm"))])
    k.hard()
    return k



def _prep_moe8(inputs):
    Wup = inputs['L1_ffn_Wup'].astype(np.float32)
    Wdn = inputs['L1_ffn_Wdown'].astype(np.float32)
    return {
        "wg1T": _t(inputs['L1_ffn_Wg1']),
        "wg2T": _t(inputs['L1_ffn_Wg2']),
        "wupT": np.ascontiguousarray(Wup.transpose(0, 2, 1)),
        "wdnT": np.ascontiguousarray(Wdn.transpose(0, 2, 1).reshape(8 * 1024, 2048)),
        "bup": _f(inputs['L1_ffn_bup']),
        "bdn": _f(inputs['L1_ffn_bdown']),
        "c_ones": np.ones(128, np.float32),
        "c_eyef": np.eye(128, dtype=np.float32),
    }


def _maps_moe8(base, xfull):
    maps = []
    for core in range(NCORES):
        b, h = core // 2, core % 2
        m = dict(base)
        m["x"] = np.ascontiguousarray(xfull[b, h * TOWN:(h + 1) * TOWN].T)
        maps.append(m)
    return maps


def _prep_moe0(inputs):
    g = lambda n: inputs['L0_ffn_' + n].astype(np.float32)
    stackT = lambda A: np.ascontiguousarray(
        np.concatenate([A[e].T for e in range(A.shape[0])], axis=0).astype(np.float32))
    return {
        "gateT": _t(g('gate_w')), "rgT": _t(g('Rg')), "ruT": _t(g('Ru')), "rdT": _t(g('Rd')),
        "ugT": stackT(g('Ug')), "cgT": stackT(g('Cg')), "uuT": stackT(g('Uu')),
        "cuT": stackT(g('Cu')), "udT": stackT(g('Ud')), "cdT": stackT(g('Cd')),
        "s_rgT": _t(g('s_Rg')), "s_ugT": _t(g('s_Ug')), "s_cgT": _t(g('s_Cg')),
        "s_ruT": _t(g('s_Ru')), "s_uuT": _t(g('s_Uu')), "s_cuT": _t(g('s_Cu')),
        "s_rdT": _t(g('s_Rd')), "s_udT": _t(g('s_Ud')), "s_cdT": _t(g('s_Cd')),
        "c_ones": np.ones(128, np.float32), "c_eyef": np.eye(128, dtype=np.float32),
    }


_H32, _DH, _BS = 32, 64, 256


def _light_consts():
    base = 1.0 / 2.0 ** (8.0 / _H32)
    rate = base ** (np.arange(_H32, dtype=np.float64) + 1.0)
    slope = (rate * (1.0 - 0.0 + 1e-5)).astype(np.float32)
    r = np.arange(_BS, dtype=np.float32) + 1.0
    qd = np.exp(-slope[:, None] * r[None, :]).astype(np.float32)
    kd = np.exp(-slope[:, None] * (_BS - r)[None, :]).astype(np.float32)
    dd_ = r[:, None] - r[None, :]
    dd = np.where(dd_ >= 0, np.exp(-slope[:, None, None] * dd_[None]), 0.0).astype(np.float32)
    ddT = np.ascontiguousarray(dd.transpose(0, 2, 1).reshape(_H32 * _BS, _BS))
    qdb = np.ascontiguousarray(np.repeat(qd[:, None, :], _DH, axis=1).reshape(_H32 * _DH, _BS))
    kdb = np.ascontiguousarray(np.repeat(kd[:, None, :], _DH, axis=1).reshape(_H32 * _DH, _BS))
    return ddT, qdb, kdb, slope


def _prep_light_a(inputs):
    qkvT = inputs['L1_attn_a_qkv_w'].astype(np.float32).T
    cols = np.arange(6144).reshape(_H32, 3, _DH)
    ddT, qdb, kdb, slope = _light_consts()
    return {
        "qT": np.ascontiguousarray(qkvT[:, cols[:, 0].ravel()]),
        "kT": np.ascontiguousarray(qkvT[:, cols[:, 1].ravel()]),
        "vT": np.ascontiguousarray(qkvT[:, cols[:, 2].ravel()]),
        "gT": _t(inputs['L1_attn_a_gate_w']),
        "ddT": ddT, "qdb": qdb, "kdb": kdb,
        "bdp": np.exp(-slope * _BS).astype(np.float32),
        "c_ones": np.ones(128, np.float32),
        "c_eyeb": np.eye(128).astype(np.float32),
    }


def _prep_light_b(inputs):
    ddT, qdb, kdb, slope = _light_consts()
    return {
        "outT": _t(inputs['L1_attn_a_out_w']),
        "normw": _f(inputs['L1_attn_a_norm_w']),
        "qdb": qdb, "c_ones": np.ones(128, np.float32),
    }


def _gqa_masks():
    avals = [-384, -256, -128, 0, 640, 768, 896]
    M = np.zeros((7, 128, 512), np.float32)
    i_ = np.arange(128)[:, None]
    j_ = np.arange(512)[None, :]
    for mi, a in enumerate(avals):
        M[mi] = ((i_ - j_ <= a) & (i_ - j_ > a - 1024)).astype(np.float32)
    return M.reshape(7 * 128, 512)


def _mla_rope(pos):
    inv = (10000.0 ** (-np.arange(0, 64, 2, dtype=np.float64) / 64))
    ang = pos.astype(np.float64)[:, None] * inv[None, :]
    return (np.cos(ang).astype(np.float32), np.sin(ang).astype(np.float32))


def _cmasks():
    cm = np.zeros((4, 128, 512), np.float32)
    i_ = np.arange(128)[:, None]
    j_ = np.arange(512)[None, :]
    for mi, a in enumerate([0, -128, -256, -384]):
        cm[mi] = (i_ - j_ <= a).astype(np.float32)
    return cm.reshape(4 * 128, 512)



_CACHE = {}


def _get_nc(name, builder):
    if name not in _CACHE:
        _CACHE[name] = builder().build()
    return _CACHE[name]


def _run(name, builder, maps):
    nc = _get_nc(name, builder)
    return run_bass_kernel_spmd(nc, maps, core_ids=list(range(NCORES)))


def kernel(**inputs):
    ones = np.ones(128, np.float32)

    x = np.asarray(inputs["L0_attn_a_x"]).astype(np.float32)
    maps = prep_rwkv(inputs, x)
    res = _run("rwkv", build_rwkv, maps)
    x = gather_fm(res)

    base_moe0 = _prep_moe0(inputs)
    res = _run("moe0", build_moe0, _maps_moe8(base_moe0, x))
    x = gather_fm(res)

    g = lambda n: inputs['L0_attn_b_' + n].astype(np.float32)
    W_KR = g('W_KR')
    baseA = {
        "wdqT": _t(g('W_DQ')), "wdkvT": _t(g('W_DKV')),
        "krTe": np.ascontiguousarray(W_KR[0::2].T),
        "krTo": np.ascontiguousarray(W_KR[1::2].T),
        "c_ones": ones,
    }
    mapsA = []
    for core in range(NCORES):
        b, h = core // 2, core % 2
        cos, sin = _mla_rope(np.arange(h * TOWN, (h + 1) * TOWN))
        m = dict(baseA)
        m["x"] = np.ascontiguousarray(x[b, h * TOWN:(h + 1) * TOWN].T)
        m["cosf"] = np.ascontiguousarray(cos.T)
        m["sinf"] = np.ascontiguousarray(sin.T)
        mapsA.append(m)
    resA = _run("mla_a", build_mla_a, mapsA)
    rA = [{kk: np.asarray(resA.results[c][kk]) for kk in ("cqd", "ckvd", "krd")}
          for c in range(NCORES)]
    cos, sin = _mla_rope(np.arange(T))
    cos4 = np.ascontiguousarray(np.tile(cos.T, (4, 1)))
    sin4 = np.ascontiguousarray(np.tile(sin.T, (4, 1)))
    cm = _cmasks()
    W_UQ, W_QR, W_UK, W_UV = g('W_UQ'), g('W_QR'), g('W_UK'), g('W_UV')
    mapsB = []
    for core in range(NCORES):
        b, h = core // 2, core % 2
        heads = slice(0, 8) if h == 0 else slice(8, 16)
        hcols = np.arange(heads.start * 128, heads.stop * 128)
        qr_rows = W_QR.reshape(16, 64, 512)[heads]
        mapsB.append({
            "cqf": np.concatenate([rA[2 * b]["cqd"], rA[2 * b + 1]["cqd"]], axis=1),
            "ckvf": np.concatenate([rA[2 * b]["ckvd"], rA[2 * b + 1]["ckvd"]], axis=1),
            "krf": np.concatenate([rA[2 * b]["krd"], rA[2 * b + 1]["krd"]], axis=1),
            "wuqT": np.ascontiguousarray(W_UQ[hcols].T),
            "wqrTe": np.ascontiguousarray(qr_rows[:, 0::2].reshape(8 * 32, 512).T),
            "wqrTo": np.ascontiguousarray(qr_rows[:, 1::2].reshape(8 * 32, 512).T),
            "wukT": np.ascontiguousarray(W_UK[hcols].T),
            "wuvT": np.ascontiguousarray(W_UV[hcols].T),
            "cos4": cos4, "sin4": sin4, "cmasks": cm, "c_ones": ones,
        })
    resB = _run("mla_b", build_mla_b, mapsB)
    rB = [np.asarray(resB.results[c]["attd"]) for c in range(NCORES)]
    woT = _t(g('W_O'))
    mapsC = []
    for core in range(NCORES):
        b, h = core // 2, core % 2
        att = np.concatenate([rB[2 * b][:, h * TOWN:(h + 1) * TOWN],
                              rB[2 * b + 1][:, h * TOWN:(h + 1) * TOWN]], axis=0)
        mapsC.append({"x": np.ascontiguousarray(x[b, h * TOWN:(h + 1) * TOWN].T),
                      "attf": np.ascontiguousarray(att), "woT": woT})
    resC = _run("mla_c", build_mla_c, mapsC)
    x = gather_fm(resC)

    res = _run("moe0", build_moe0, _maps_moe8(base_moe0, x))
    x = gather_fm(res)

    baseLA = _prep_light_a(inputs)
    resLA = _run("light_a", build_light_a, _maps_moe8(baseLA, x))
    rLA = [{kk: np.asarray(resLA.results[c][kk])
            for kk in ("attnr", "qd", "gd", "statec")} for c in range(NCORES)]
    baseLB = _prep_light_b(inputs)
    mapsLB = []
    for core in range(NCORES):
        b, h = core // 2, core % 2
        m = dict(baseLB)
        m["x"] = np.ascontiguousarray(x[b, h * TOWN:(h + 1) * TOWN].T)
        m["attnr"] = rLA[core]["attnr"]
        m["qd"] = rLA[core]["qd"]
        m["gd"] = rLA[core]["gd"]
        m["statec"] = (rLA[2 * b]["statec"] if h == 1 else
                       np.zeros((128, 16 * _DH), np.float32))
        mapsLB.append(m)
    resLB = _run("light_b", build_light_b, mapsLB)
    x = gather_fm(resLB)

    base_moe8 = _prep_moe8(inputs)
    res = _run("moe8", build_moe8, _maps_moe8(base_moe8, x))
    x = gather_fm(res)

    baseGA = {
        "wq": _f(inputs['L1_attn_b_Wq']), "wk": _f(inputs['L1_attn_b_Wk']),
        "wv": _f(inputs['L1_attn_b_Wv']), "c_ones": ones,
    }
    resGA = _run("gqa_a", build_gqa_a, _maps_moe8(baseGA, x))
    rGA = [{kk: np.asarray(resGA.results[c][kk]) for kk in ("qd", "kd", "vd")}
           for c in range(NCORES)]
    baseGB = {
        "wo": _f(inputs['L1_attn_b_Wo']), "bo": _f(inputs['L1_attn_b_bo']),
        "masks": _gqa_masks(), "c_ones": ones,
    }
    mapsGB = []
    for core in range(NCORES):
        b, h = core // 2, core % 2
        m = dict(baseGB)
        m["x"] = np.ascontiguousarray(x[b, h * TOWN:(h + 1) * TOWN].T)
        m["qd"] = rGA[core]["qd"]
        k0, k1 = rGA[2 * b]["kd"], rGA[2 * b + 1]["kd"]
        v0, v1 = rGA[2 * b]["vd"], rGA[2 * b + 1]["vd"]
        if h == 1:
            m["kf"] = np.ascontiguousarray(np.concatenate([k0, k1], axis=1))
            m["vf"] = np.ascontiguousarray(np.concatenate([v0, v1], axis=0))
            m["kvalid"] = np.ones((128, 16), np.float32)
        else:
            m["kf"] = np.ascontiguousarray(np.concatenate([np.zeros_like(k0), k0], axis=1))
            m["vf"] = np.ascontiguousarray(np.concatenate([np.zeros_like(v0), v0], axis=0))
            m["kvalid"] = np.zeros((128, 16), np.float32)
        mapsGB.append(m)
    resGB = _run("gqa_b", build_gqa_b, mapsGB)
    x = gather_fm(resGB)

    res = _run("moe8", build_moe8, _maps_moe8(base_moe8, x))
    x = gather_fm(res)

    return x
